# version 05
# baseline (speedup 1.0000x reference)
import numpy as np
from contextlib import ExitStack
from types import SimpleNamespace

import concourse.bass as bass
import concourse.mybir as mybir
from concourse.bass_utils import run_bass_kernel_spmd

F32 = mybir.dt.float32
BF16 = mybir.dt.bfloat16
AXX = mybir.AxisListType.X
ALU = mybir.AluOpType
ACTF = mybir.ActivationFunctionType

R = 8
T = 1024
D = 2048
DC = 16
OQ = 48
DFF = 1408
FC = 11
NE = 18
SG = 8192
NJC = 16
NIT = 8
NM = NJC * NIT
MODW = 12288
MODSL = MODW // R
SCALE = float(1.0 / np.sqrt(2048.0))
EPS = 1e-5
BIGNEG = -1.0e30


class Sched:
    def __init__(self):
        self.c = {}
        self.n = {}
        self.frozen = False

    def reset_counts(self):
        self.n = {}

    def inc(self, sem, labels=(), by=1):
        v = self.n.get(sem, 0) + by
        self.n[sem] = v
        for L in labels:
            key = (sem, L)
            if self.frozen:
                assert self.c.get(key) == v, f"sched mismatch {key}: {self.c.get(key)} vs {v}"
            else:
                assert key not in self.c, f"dup label {key}"
                self.c[key] = v
        return v

    def at(self, sem, label):
        return self.c[(sem, label)]


class _DryInst:
    def then_inc(self, *a, **k):
        return self


_DRY_INST = _DryInst()


class _DryEng:
    def __getattr__(self, n):
        if n == "wait_ge":
            return lambda *a, **k: None
        return lambda *a, **k: _DRY_INST


class _DryAP:
    def __getitem__(self, *_):
        return self

    def rearrange(self, *a, **k):
        return self

    def opt(self):
        return self


_DRY_AP = _DryAP()


class _DryB:
    def __getattr__(self, n):
        return _DRY_AP


def emit(nc, ctx, dr, sems, S, dry, blk=None):
    S.reset_counts()

    if dry:
        B = _DryB()
    else:
        B = SimpleNamespace()

    def alloc(st, name, shape, dt, attr=None):
        if not dry:
            setattr(B, attr or name, st.enter_context(nc.sbuf_tensor(name, shape, dt))[:])

    def palloc(st, name, shape, dt):
        if not dry:
            setattr(B, name, st.enter_context(nc.psum_tensor(name, shape, dt))[:])

    def inc(ins, sem, *labels, by=1):
        S.inc(sem, labels, by)
        if not dry and ins is not None:
            ins.then_inc(sems[sem], by)

    def wt(eng, sem, label):
        if dry:
            return
        eng.wait_ge(sems[sem], S.at(sem, label))

    def wtv(eng, sem, v):
        if dry:
            return
        eng.wait_ge(sems[sem], v)

    def drain(eng, ins):
        inc(ins, "dv")
        wtv(eng, "dv", S.n["dv"])

    class Chain:

        def __init__(self, q, semname):
            self._q = q
            self._sem = semname

        def dma_start(self, **kw):
            prev = S.n.get(self._sem, 0)
            if prev and not dry:
                self._q.wait_ge(sems[self._sem], prev)
            return self._q.dma_start(**kw)

        def __getattr__(self, n):
            return getattr(self._q, n)

    def run_phase(fns):
        order = ("sync", "gpsimd", "tensor", "scalar", "vector")
        for name in order:
            if name in fns:
                if dry:
                    fns[name](_DryEng())
                else:
                    getattr(blk, name)(fns[name])

    if not dry:
        per = ctx
    alloc(ctx, "sH", [128, DC * T], BF16)
    alloc(ctx, "sQT", [128, DC * T], BF16)
    alloc(ctx, "sMODT", [128, 96], F32)

    def alloc_pack(st, name, cols, views, dt):
        if not dry:
            h = st.enter_context(nc.sbuf_tensor(name, [128, cols], dt))[:]
            for vn, c0, c1 in views:
                setattr(B, vn, h[:, c0:c1])

    alloc_pack(ctx, "sSM16", 144, [
        ("sCA1", 0, 16), ("sCB1", 16, 32), ("sGA", 32, 48), ("sGB", 48, 64),
        ("sGAB", 64, 80), ("sGBS2", 80, 96), ("sSB2S", 96, 112),
        ("sT16a", 112, 128), ("sT16b", 128, 144)], F32)
    alloc_pack(ctx, "sPKB", 310, [
        ("sBQKV", 0, 48), ("sBOUT", 48, 64), ("sSB1", 64, 86),
        ("sRB1", 86, 262), ("sSB22", 262, 294), ("sC2F", 294, 310)], F32)
    alloc_pack(ctx, "sFST", 56, [
        ("sMN", 0, 8), ("sMJ", 8, 16), ("sMD", 16, 24), ("sAL", 24, 32),
        ("sLL", 32, 40), ("sRS", 40, 48), ("sRL", 48, 56)], F32)
    alloc(ctx, "sONESR", [1, 128], F32)
    alloc(ctx, "sONESC", [128, 1], F32)
    alloc(ctx, "sONESCB", [128, 1], BF16)
    alloc(ctx, "sIDF", [128, 128], F32)
    alloc(ctx, "sIDB", [128, 128], BF16)
    alloc(ctx, "sFB", [16, 1], F32)
    alloc(ctx, "sCENTB", [128, DC * 16], BF16)
    alloc(ctx, "sSCB", [128, 16], F32)
    pp = []
    for i in range(7):
        if not dry:
            pp.append(ctx.enter_context(nc.psum_tensor(f"pp{i}", [128, 512], F32))[:])
        else:
            pp.append(_DRY_AP)
    if not dry:
        pb = ctx.enter_context(nc.psum_tensor("pb", [128, 1024], BF16))[:]
    else:
        pb = _DRY_AP

    y_ap = dr["y"]

    st0 = ExitStack()
    if not dry:
        st0.__enter__()
    alloc(st0, "sX", [128, DC * T], BF16)
    alloc(st0, "sWM", [128, 2 * MODSL], F32)
    alloc(st0, "sXSQ", [128, 2 * 512], F32)
    alloc(st0, "sKST", [128, 2 * T], BF16)
    alloc(st0, "sVAS", [128, 8 * D], BF16)
    alloc(st0, "sWQ", [128, 2 * DC * 128], BF16)
    alloc(st0, "sMROW", [1, MODSL], F32)
    alloc(st0, "sMLD", [96, 128], F32)
    alloc(st0, "sBQR", [1, 2048], F32)
    alloc(st0, "sSTAT", [1, 6144], F32)
    alloc(st0, "sTMP0", [128, 512], F32, attr="sTMP")

    def p0_sync(q):
        q = Chain(q, "hds")
        smalls = (
            (B.sONESR, dr["ones"]), (B.sONESC, dr["ones_col"]), (B.sIDF, dr["ident"]),
            (B.sC2F, dr["c2"]), (B.sBQKV, dr["bqkv2"]), (B.sBOUT, dr["bout2"]),
            (B.sSB1, dr["sb1_2"]), (B.sRB1, dr["rb1_2"]), (B.sSB22, dr["sb2_2"]),
            (B.sFB, dr["fbias"]), (B.sBQR, dr["bqvr"]),
        )
        for i, (out, src) in enumerate(smalls):
            ins = q.dma_start(out=out, in_=src)
            inc(ins, "hds", *(("csmall",) if i == len(smalls) - 1 else ()), by=16)
        wt(q, "dv", "modrow")
        ins = q.dma_start(out=dr["mod_in"], in_=B.sMROW)
        inc(ins, "hds", "modin", by=16)
        wtv(q, "cc1", 1)
        ins = q.dma_start(
            out=B.sMLD,
            in_=dr["mod_all"].rearrange("r (c p) -> (r c) p", p=128),
        )
        inc(ins, "hds", "modall", by=16)
        for kc in range(16):
            wt(q, "dv", ("kev", kc, 1))
            ins = q.dma_start(out=dr["k_in"][kc * 128:(kc + 1) * 128, :],
                              in_=B.sKST[:, (kc % 2) * T:(kc % 2) * T + T])
            inc(ins, "hds", ("kin", kc), by=16)
        wt(q, "ac", ("vevB", 47))
        for it in range(8):
            ins = q.dma_start(out=dr["v_in"][it * 128:(it + 1) * 128, :],
                              in_=B.sVAS[:, it * D:(it + 1) * D])
            inc(ins, "hds", ("vin", it), by=16)

    def p0_gpsimd(g):
        g = Chain(g, "ldg")
        ins = g.dma_start(
            out=B.sX.rearrange("p (c t) -> p c t", t=T),
            in_=dr["xT"].rearrange("(c p) t -> p c t", p=128),
        )
        inc(ins, "ldg", "xload", by=16)
        ins = g.dma_start(out=B.sONESCB, in_=dr["ones_col"])
        inc(ins, "ldg", "onescb", by=16)
        ins = g.dma_start(out=B.sIDB, in_=dr["ident"])
        inc(ins, "ldg", "idb", by=16)
        ins = g.dma_start(
            out=B.sCENTB.rearrange("p (c e) -> p c e", e=16),
            in_=dr["centT"].rearrange("(c p) e -> p c e", p=128),
        )
        inc(ins, "ldg", "cent", by=16)
        for kc in range(16):
            if kc >= 2:
                wt(g, "pe", ("mod", kc - 2))
            ins = g.dma_start(out=B.sWM[:, (kc % 2) * MODSL:(kc % 2 + 1) * MODSL],
                              in_=dr["wmod_sl"][kc * 128:(kc + 1) * 128, :])
            inc(ins, "ldg", ("wm", kc), by=16)
        wt(g, "hds", "modin")
        ins = g.collective_compute(
            "AllGather", ALU.bypass, replica_groups=[list(range(R))],
            ins=[dr["mod_in"].opt()], outs=[dr["mod_all"].opt()],
        )
        inc(ins, "cc1", "modag")
        for oc in range(OQ):
            if oc >= 2:
                wt(g, "pe", ("qkv", oc - 2))
            ins = g.dma_start(
                out=B.sWQ[:, (oc % 2) * DC * 128:(oc % 2 + 1) * DC * 128].rearrange(
                    "p (c f) -> p c f", f=128),
                in_=dr["wqkvT"][:, oc * 128:(oc + 1) * 128].rearrange(
                    "(c p) f -> p c f", p=128),
            )
            inc(ins, "ldg", ("wq", oc), by=16)
        wt(g, "hds", ("kin", 15))
        wt(g, "hds", ("vin", 7))
        ins = g.collective_compute(
            "AllGather", ALU.bypass, replica_groups=[list(range(R))],
            ins=[dr["k_in"].opt()], outs=[dr["k_all"].opt()],
        )
        inc(ins, "cc2", "kag")
        ins = g.collective_compute(
            "AllGather", ALU.bypass, replica_groups=[list(range(R))],
            ins=[dr["v_in"].opt()], outs=[dr["v_all"].opt()],
        )
        inc(ins, "cc3", "vag")

    def p0_tensor(t):
        wt(t, "ac", "silu")
        for kc in range(16):
            wt(t, "ldg", ("wm", kc))
            ins = None
            for os in range(3):
                ins = t.matmul(pp[os][0:1, :], B.sSCB[:, kc:kc + 1],
                               B.sWM[:, (kc % 2) * MODSL + os * 512:(kc % 2) * MODSL + (os + 1) * 512],
                               start=(kc == 0), stop=(kc == 15))
            inc(ins, "pe", ("mod", kc))
        wt(t, "hds", "modall")
        ins = t.transpose(pp[6][:, 0:96], B.sMLD, B.sIDF[0:96, 0:96])
        inc(ins, "pe", "modT")
        wt(t, "ldg", "onescb")
        wt(t, "dv", "modtev")
        for tch in range(2):
            for dc in range(DC):
                k = tch * DC + dc
                xs = B.sX[:, dc * T + tch * 512: dc * T + tch * 512 + 512]
                ins = t.matmul(pp[3 + tch][0:1, :], B.sONESCB, xs,
                               start=(dc == 0), stop=(dc == DC - 1))
                wt(t, "ac", ("xsq1", k))
                ins = t.matmul(pp[5 + tch][0:1, :], B.sONESC,
                               B.sXSQ[:, (k % 2) * 512:(k % 2) * 512 + 512],
                               start=(dc == 0), stop=(dc == DC - 1))
                inc(ins, "pe", ("sqmm1", k))
        inc(None, "pe", "stats1", by=0) if False else None
        wt(t, "dv", "ab1")
        wt(t, "dv", "modev")
        bank_ab1 = {(0, 0): 0, (0, 1): 1, (1, 0): 2, (1, 1): 5}
        ins = None
        for which in range(2):
            for tch in range(2):
                ins = t.matmul(pp[bank_ab1[(which, tch)]][:, :], B.sONESR[0:1, 0:128],
                               B.sSTAT[0:1, 3072 + which * 1024 + tch * 512: 3072 + which * 1024 + (tch + 1) * 512],
                               start=True, stop=True)
        inc(ins, "pe", "bc1")
        wt(t, "dv", "hdone1")
        for oc in range(OQ):
            wt(t, "ldg", ("wq", oc))
            wslab = B.sWQ[:, (oc % 2) * DC * 128:(oc % 2 + 1) * DC * 128]
            if oc < 32:
                for tch in range(2):
                    bank = (oc * 2 + tch) % 4
                    if oc >= 2:
                        if oc < 16:
                            wt(t, "dv", ("qev", oc - 2, tch))
                        elif oc == 16 or oc == 17:
                            wt(t, "dv", ("qev", oc - 2, tch))
                        else:
                            wt(t, "dv", ("kev", oc - 2 - 16, tch))
                    ins = None
                    for dc in range(DC):
                        ins = t.matmul(pp[bank][:, :], wslab[:, dc * 128:(dc + 1) * 128],
                                       B.sH[:, dc * T + tch * 512: dc * T + tch * 512 + 512],
                                       start=(dc == 0), stop=(dc == DC - 1))
                    inc(ins, "pe", ("qk", oc, tch), *((("qkv", oc),) if tch == 1 else ()))
            else:
                for it in range(8):
                    half = 4 if it < 4 else 5
                    qtr = it % 4
                    if it == 0:
                        wt(t, "ac", ("vevA", oc - 1)) if oc > 32 else wt(t, "dv", "hdone1")
                    if it == 4:
                        wt(t, "ac", ("vevB", oc - 1)) if oc > 32 else None
                        if oc == 32:
                            wt(t, "dv", "hdone1")
                    ins = None
                    for dc in range(DC):
                        ins = t.matmul(pp[half][:, qtr * 128:(qtr + 1) * 128],
                                       B.sH[:, dc * T + it * 128: dc * T + it * 128 + 128],
                                       wslab[:, dc * 128:(dc + 1) * 128],
                                       start=(dc == 0), stop=False)
                    ins = t.matmul(pp[half][:, qtr * 128:(qtr + 1) * 128],
                                   B.sONESR[0:1, 0:128],
                                   B.sBQR[0:1, oc * 128 - 4096:(oc + 1) * 128 - 4096],
                                   start=False, stop=True)
                    if it == 3:
                        inc(ins, "pe", ("qkA", oc))
                    elif it == 7:
                        inc(ins, "pe", ("qkB", oc), ("qkv", oc))
                    else:
                        inc(ins, "pe")
        inc(None, "pe", by=0) if False else None

    def p0_scalar(s):
        wt(s, "hds", "csmall")
        ins = s.activation(out=B.sSCB, in_=B.sC2F, func=ACTF.Silu)
        inc(ins, "ac", "silu")
        wt(s, "ldg", "xload")
        for tch in range(2):
            for dc in range(DC):
                k = tch * DC + dc
                if k >= 2:
                    wt(s, "pe", ("sqmm1", k - 2))
                ins = s.activation(out=B.sXSQ[:, (k % 2) * 512:(k % 2) * 512 + 512],
                                   in_=B.sX[:, dc * T + tch * 512: dc * T + tch * 512 + 512],
                                   func=ACTF.Square)
                inc(ins, "ac", ("xsq1", k))
        wt(s, "dv", "var1")
        ins = s.activation(out=B.sSTAT[0:1, 2048:3072], in_=B.sSTAT[0:1, 4096:5120],
                           func=ACTF.Sqrt)
        inc(ins, "ac", "sqrt1")
        for oc in range(32, OQ):
            wt(s, "pe", ("qkA", oc))
            if oc >= 34:
                wt(s, "hds", ("vin", 7)) if False else None
            ins = s.activation(
                out=B.sVAS.rearrange("p (a d) -> p a d", d=D)[:, 0:4, oc * 128 - 4096: oc * 128 - 4096 + 128],
                in_=pp[4].rearrange("p (a q) -> p a q", q=128),
                func=ACTF.Copy)
            inc(ins, "ac", ("vevA", oc))
            wt(s, "pe", ("qkB", oc))
            ins = s.activation(
                out=B.sVAS.rearrange("p (a d) -> p a d", d=D)[:, 4:8, oc * 128 - 4096: oc * 128 - 4096 + 128],
                in_=pp[5].rearrange("p (a q) -> p a q", q=128),
                func=ACTF.Copy)
            inc(ins, "ac", ("vevB", oc))

    def p0_vector(v):
        wt(v, "pe", ("mod", 15))
        ins = None
        for os in range(3):
            ins = v.tensor_copy(B.sMROW[0:1, os * 512:(os + 1) * 512], pp[os][0:1, :])
        inc(ins, "dv", "modrow", "modev")
        wt(v, "pe", "modT")
        ins = v.tensor_copy(B.sMODT, pp[6][:, 0:96])
        inc(ins, "dv", "modtev")
        wt(v, "hds", "csmall")
        v.tensor_scalar(B.sCA1, B.sMODT[:, 16:32], 1.0, None, ALU.add) if not dry else None
        v.tensor_scalar(B.sCB1, B.sMODT[:, 64:80], 1.0, None, ALU.add) if not dry else None
        for (src0, dstname) in ((32, "sGA"), (80, "sGB")):
            gsl = B.sMODT[:, src0:src0 + 16]
            dst = getattr(B, dstname) if not dry else None
            ins = v.tensor_single_scalar(B.sT16a, gsl, 0.0, ALU.is_ge) if not dry else None
            drain(v, ins)
            ins = v.tensor_scalar(B.sT16a, B.sT16a, 2.0, -1.0, ALU.mult, ALU.add) if not dry else None
            drain(v, ins)
            ins = v.tensor_scalar(B.sT16b, gsl, -1.0, None, ALU.mult) if not dry else None
            drain(v, ins)
            ins = v.tensor_max(B.sT16b, B.sT16b, gsl) if not dry else None
            drain(v, ins)
            ins = v.tensor_scalar_max(B.sT16b, B.sT16b, 0.25) if not dry else None
            drain(v, ins)
            ins = v.tensor_mul(dst, B.sT16a, B.sT16b) if not dry else None
            drain(v, ins)
        ins = v.tensor_mul(B.sGAB, B.sGA, B.sBOUT) if not dry else None
        drain(v, ins)
        ins = v.tensor_add(B.sSB2S, B.sSB22[:, 0:16], B.sSB22[:, 16:32]) if not dry else None
        drain(v, ins)
        ins = v.tensor_mul(B.sGBS2, B.sGB, B.sSB2S) if not dry else None
        inc(ins, "dv", "modderiv")
        wt(v, "pe", ("sqmm1", 31))
        if not dry:
            v.tensor_scalar(B.sSTAT[0:1, 0:512], pp[3][0:1, :], 1.0 / D, None, ALU.mult)
            v.tensor_scalar(B.sSTAT[0:1, 512:1024], pp[4][0:1, :], 1.0 / D, None, ALU.mult)
            v.tensor_scalar(B.sSTAT[0:1, 3072:3584], pp[5][0:1, :], 1.0 / D, None, ALU.mult)
            v.tensor_scalar(B.sSTAT[0:1, 3584:4096], pp[6][0:1, :], 1.0 / D, None, ALU.mult)
            v.tensor_mul(B.sSTAT[0:1, 4096:5120], B.sSTAT[0:1, 0:1024], B.sSTAT[0:1, 0:1024])
            v.tensor_sub(B.sSTAT[0:1, 1024:2048], B.sSTAT[0:1, 3072:4096], B.sSTAT[0:1, 4096:5120])
            v.tensor_scalar(B.sSTAT[0:1, 1024:2048], B.sSTAT[0:1, 1024:2048], EPS, None, ALU.add)
        ins = v.reciprocal(B.sSTAT[0:1, 4096:5120], B.sSTAT[0:1, 1024:2048]) if not dry else None
        inc(ins, "dv", "var1")
        wt(v, "ac", "sqrt1")
        if not dry:
            r0 = B.sSTAT[0:1, 2048:3072]
            v.tensor_mul(B.sSTAT[0:1, 5120:6144], r0, r0)
            v.tensor_mul(B.sSTAT[0:1, 5120:6144], B.sSTAT[0:1, 5120:6144], B.sSTAT[0:1, 1024:2048])
            v.tensor_scalar(B.sSTAT[0:1, 5120:6144], B.sSTAT[0:1, 5120:6144], -0.5, 1.5, ALU.mult, ALU.add)
            v.tensor_mul(B.sSTAT[0:1, 3072:4096], B.sSTAT[0:1, 5120:6144], r0)
            v.scalar_tensor_tensor(B.sSTAT[0:1, 4096:5120], B.sSTAT[0:1, 0:1024], -1.0,
                                   B.sSTAT[0:1, 3072:4096], ALU.mult, ALU.mult)
        ins = v.tensor_copy(B.sSTAT[0:1, 5120:5121], B.sSTAT[0:1, 3072:3073]) if not dry else None
        inc(ins, "dv", "ab1")
        wt(v, "pe", "bc1")
        bank_ab1 = {(0, 0): 0, (0, 1): 1, (1, 0): 2, (1, 1): 5}
        ins = None
        for dc in range(DC):
            for tch in range(2):
                xs = B.sX[:, dc * T + tch * 512: dc * T + tch * 512 + 512]
                if not dry:
                    v.tensor_mul(B.sTMP, xs, pp[bank_ab1[(0, tch)]][:, :])
                    v.tensor_add(B.sTMP, B.sTMP, pp[bank_ab1[(1, tch)]][:, :])
                ins = v.tensor_scalar(B.sH[:, dc * T + tch * 512: dc * T + tch * 512 + 512],
                                      B.sTMP, B.sCA1[:, dc:dc + 1],
                                      B.sMODT[:, dc:dc + 1], ALU.mult, ALU.add) if not dry else None
        inc(ins, "dv", "hdone1")
        for oc in range(32):
            for tch in range(2):
                wt(v, "pe", ("qk", oc, tch))
                bank = (oc * 2 + tch) % 4
                if oc < 16:
                    ins = v.tensor_scalar(B.sQT[:, oc * T + tch * 512: oc * T + tch * 512 + 512],
                                          pp[bank][:, :], B.sBQKV[:, oc:oc + 1], SCALE,
                                          ALU.add, ALU.mult) if not dry else None
                    inc(ins, "dv", ("qev", oc, tch))
                else:
                    kc = oc - 16
                    if kc >= 2:
                        wt(v, "hds", ("kin", kc - 2))
                    ins = v.tensor_scalar(B.sKST[:, (kc % 2) * T + tch * 512:(kc % 2) * T + tch * 512 + 512],
                                          pp[bank][:, :], B.sBQKV[:, oc:oc + 1], None,
                                          ALU.add) if not dry else None
                    inc(ins, "dv", ("kev", kc, tch))

    run_phase({"sync": p0_sync, "gpsimd": p0_gpsimd, "tensor": p0_tensor,
               "scalar": p0_scalar, "vector": p0_vector})
    if not dry:
        st0.close()

    st1 = ExitStack()
    if not dry:
        st1.__enter__()
    alloc(st1, "sCTX", [128, 8 * D], F32)
    alloc(st1, "sKB", [128, 2 * DC * 512], BF16)
    alloc(st1, "sVB", [128, 2 * 4 * D], BF16)
    alloc(st1, "sPRB", [128, 2 * 512], BF16)
    alloc(st1, "sPT", [128, 2 * 512], BF16)

    def p1_sync(q):
        q = Chain(q, "hds")
        wtv(q, "cc2", 1)
        wtv(q, "cc3", 1)
        wt(q, "pe", "p0pe")
        if "dbg_q" in dr:
            wt(q, "dv", "p0dv")
            ins = q.dma_start(out=dr["dbg_q"], in_=B.sQT)
            inc(ins, "hds", by=16)
        dbg_kb = "dbg_kb" in dr
        for jc in range(NJC):
            if jc >= 2:
                wt(q, "pe", ("fpv", (jc - 1) * 8 - 1))
            rr, jloc = jc // 2, (jc % 2) * 512
            ins = q.dma_start(
                out=B.sKB[:, (jc % 2) * DC * 512:(jc % 2 + 1) * DC * 512].rearrange(
                    "p (c t) -> p c t", t=512),
                in_=dr["k_all"][rr * D:(rr + 1) * D, jloc:jloc + 512].rearrange(
                    "(c p) t -> p c t", p=128),
            )
            inc(ins, "hds", ("kb", jc), by=16)
            ins = q.dma_start(
                out=B.sVB[:, (jc % 2) * 4 * D:(jc % 2 + 1) * 4 * D].rearrange(
                    "p (a d) -> p a d", d=D),
                in_=dr["v_all"][jc * 512:(jc + 1) * 512, :].rearrange(
                    "(a p) d -> p a d", p=128),
            )
            inc(ins, "hds", ("vb", jc), by=16)
            if dbg_kb and jc == 0:
                ins = q.dma_start(out=dr["dbg_kb"], in_=B.sKB[:, 0:DC * 512])
                inc(ins, "hds", by=16)
                ins = q.dma_start(out=dr["dbg_vb"], in_=B.sVB[:, 0:4 * D])
                inc(ins, "hds", by=16)
                wt(q, "dv", ("fstat", 0))
                ins = q.dma_start(out=dr["dbg_st0"], in_=B.sMN[:, 0:1])
                inc(ins, "hds", by=16)
                ins = q.dma_start(out=dr["dbg_st1"], in_=B.sMD[:, 0:1])
                inc(ins, "hds", by=16)
                wt(q, "ac", ("pexp", 0))
                ins = q.dma_start(out=dr["dbg_pr"], in_=B.sPRB[:, 0:512])
                inc(ins, "hds", by=16)
                ins = q.dma_start(out=dr["dbg_st2"], in_=B.sRS[:, 0:1])
                inc(ins, "hds", by=16)
                ins = q.dma_start(out=dr["dbg_st3"], in_=B.sAL[:, 0:1])
                inc(ins, "hds", by=16)
            if dbg_kb and jc == 1:
                wt(q, "ac", ("pexp", 8))
                ins = q.dma_start(out=dr["dbg_st4"], in_=B.sMN[:, 0:1])
                inc(ins, "hds", by=16)
                ins = q.dma_start(out=dr["dbg_st5"], in_=B.sAL[:, 0:1])
                inc(ins, "hds", by=16)
                ins = q.dma_start(out=dr["dbg_st6"], in_=B.sRS[:, 0:1])
                inc(ins, "hds", by=16)
                ins = q.dma_start(out=dr["dbg_st7"], in_=B.sPRB[:, 0:512])
                inc(ins, "hds", by=16)

    def p1_tensor(t):
        wt(t, "ac", "p0ac")
        wt(t, "dv", "p0dv")
        for m in range(NM + 1):
            jc, it = divmod(m, 8)
            if m < NM:
                if m < 8 or it == 0:
                    wt(t, "hds", ("kb", jc))
                if m >= 2:
                    wt(t, "ac", ("pexp", m - 2))
                ins = None
                for dc in range(DC):
                    ins = t.matmul(pp[m % 2][:, :],
                                   B.sQT[:, dc * T + it * 128: dc * T + it * 128 + 128],
                                   B.sKB[:, (jc % 2) * DC * 512 + dc * 512:(jc % 2) * DC * 512 + (dc + 1) * 512],
                                   start=(dc == 0), stop=(dc == DC - 1))
                inc(ins, "pe", ("fsc", m))
            if m >= 1:
                mp = m - 1
                jcp, itp = divmod(mp, 8)
                wt(t, "ac", ("pexp", mp))
                if mp >= 1:
                    wt(t, "dv", ("ptev", mp - 1))
                ins = None
                for qtr in range(4):
                    ins = t.transpose(pb[:, qtr * 128:(qtr + 1) * 128],
                                      B.sPRB[:, (mp % 2) * 512 + qtr * 128:(mp % 2) * 512 + (qtr + 1) * 128],
                                      B.sIDB)
                inc(ins, "pe", ("ftr", mp))
                wt(t, "dv", ("ptev", mp))
                if mp >= 1:
                    wt(t, "dv", ("fpvev", mp - 1))
                if mp < 8 or itp == 0:
                    wt(t, "hds", ("vb", jcp))
                for osl in range(4):
                    ins = None
                    for jsub in range(4):
                        ins = t.matmul(pp[2 + osl][:, :],
                                       B.sPT[:, (mp % 2) * 512 + jsub * 128:(mp % 2) * 512 + (jsub + 1) * 128],
                                       B.sVB[:, (jcp % 2) * 4 * D + jsub * D + osl * 512:(jcp % 2) * 4 * D + jsub * D + (osl + 1) * 512],
                                       start=(jsub == 0), stop=(jsub == 3))
                    if osl == 3:
                        inc(ins, "pe", ("fpv", mp))

    def p1_scalar(s):
        wt(s, "dv", "p0dv")
        for m in range(NM):
            jc, it = divmod(m, 8)
            wt(s, "dv", ("fstat", m))
            if m >= 2:
                wt(s, "pe", ("ftr", m - 2))
            ins = s.activation(out=B.sAL[:, it:it + 1], in_=B.sMD[:, it:it + 1], func=ACTF.Exp)
            ins = s.activation(out=B.sPRB[:, (m % 2) * 512:(m % 2) * 512 + 512],
                               in_=pp[m % 2][:, :], func=ACTF.Exp,
                               bias=B.sMN[:, it:it + 1],
                               accum_out=B.sRS[:, it:it + 1])
            inc(ins, "ac", ("pexp", m))

    def p1_vector(v):
        wt(v, "pe", "p0pe")
        wt(v, "hds", ("vin", 7))
        if not dry:
            v.memset(B.sMN, 3.0e38)
            v.memset(B.sLL, 0.0)
            v.memset(B.sCTX, 0.0)
        ins = v.memset(B.sRS, 0.0) if not dry else None
        inc(ins, "dv", "p1init")
        for m in range(NM):
            jc, it = divmod(m, 8)
            wt(v, "pe", ("fsc", m))
            ins = v.reduce_max(B.sMJ[:, it:it + 1], pp[m % 2][:, :], axis=AXX, negate=True) if not dry else None
            drain(v, ins)
            ins = v.tensor_tensor(B.sMD[:, it:it + 1], B.sMJ[:, it:it + 1], B.sMN[:, it:it + 1], ALU.min) if not dry else None
            drain(v, ins)
            ins = v.tensor_sub(B.sMJ[:, it:it + 1], B.sMD[:, it:it + 1], B.sMN[:, it:it + 1]) if not dry else None
            drain(v, ins)
            if not dry:
                v.tensor_copy(B.sMN[:, it:it + 1], B.sMD[:, it:it + 1])
            ins = v.tensor_copy(B.sMD[:, it:it + 1], B.sMJ[:, it:it + 1]) if not dry else None
            inc(ins, "dv", ("fstat", m))
            wt(v, "ac", ("pexp", m))
            ins = v.scalar_tensor_tensor(B.sLL[:, it:it + 1], B.sLL[:, it:it + 1],
                                         B.sAL[:, it:it + 1], B.sRS[:, it:it + 1],
                                         ALU.mult, ALU.add) if not dry else None
            inc(ins, "dv")
            wt(v, "pe", ("ftr", m))
            if m >= 2:
                wt(v, "pe", ("fpv", m - 1))
            ins = v.tensor_copy(B.sPT[:, (m % 2) * 512:(m % 2) * 512 + 512], pb[:, 0:512]) if not dry else None
            inc(ins, "dv", ("ptev", m))
            wt(v, "pe", ("fpv", m))
            ins = None
            for osl in range(4):
                ins = v.scalar_tensor_tensor(
                    B.sCTX[:, it * D + osl * 512: it * D + (osl + 1) * 512],
                    B.sCTX[:, it * D + osl * 512: it * D + (osl + 1) * 512],
                    B.sAL[:, it:it + 1], pp[2 + osl][:, :],
                    ALU.mult, ALU.add) if not dry else None
            inc(ins, "dv", ("fpvev", m))
        ins = v.reciprocal(B.sRL, B.sLL) if not dry else None
        drain(v, ins)
        ins = None
        for it in range(8):
            ins = v.tensor_scalar(B.sQT[:, it * D:(it + 1) * D],
                                  B.sCTX[:, it * D:(it + 1) * D],
                                  B.sRL[:, it:it + 1], None, ALU.mult) if not dry else None
        inc(ins, "dv", "ctxb")

    run_phase({"sync": p1_sync, "tensor": p1_tensor, "scalar": p1_scalar,
               "vector": p1_vector})
    if not dry:
        st1.close()

    st2a = ExitStack()
    if not dry:
        st2a.__enter__()
    alloc(st2a, "sX2", [128, DC * T], F32)
    alloc(st2a, "sW1B", [128, 2 * DC * 128], BF16)
    alloc(st2a, "sW2B", [128, 2 * FC * 128], BF16)
    if not dry:
        B.sWO = B.sW1B
    st2b = ExitStack()
    if not dry:
        st2b.__enter__()
    alloc(st2b, "sCTXT", [128, DC * T], BF16)
    alloc_pack(st2b, "sXT2", 1024, [("sXSQ2", 0, 1024), ("sTMP", 0, 512)], F32)
    alloc(st2b, "sSTAT2", [1, 6144], F32, attr="sSTAT")

    def p2_sync(q):
        q = Chain(q, "hds")
        wt(q, "dv", "ctxb")
        wt(q, "pe", ("fpv", NM - 1))
        ins = q.dma_start(
            out=B.sX2.rearrange("p (c t) -> p c t", t=T),
            in_=dr["xT"].rearrange("(c p) t -> p c t", p=128),
        )
        inc(ins, "hds", "xload2", by=16)
        if "dbg_x2" in dr:
            wt(q, "dv", ("opjev", 15, 1))
            ins = q.dma_start(out=dr["dbg_x2"], in_=B.sX2)
            inc(ins, "hds", "dbgx2", by=16)

    def p2_gpsimd(g):
        g = Chain(g, "ldg")
        wt(g, "dv", "ctxb")
        wt(g, "pe", ("fpv", NM - 1))
        for oc2 in range(16):
            if oc2 >= 2:
                wt(g, "pe", ("oproj", oc2 - 2))
            ins = g.dma_start(
                out=B.sWO[:, (oc2 % 2) * DC * 128:(oc2 % 2 + 1) * DC * 128].rearrange(
                    "p (c f) -> p c f", f=128),
                in_=dr["woutT"][:, oc2 * 128:(oc2 + 1) * 128].rearrange(
                    "(c p) f -> p c f", p=128),
            )
            inc(ins, "ldg", ("wo", oc2), by=16)
        for e in range(NE + 1):
            if e < NE:
                for fc in range(FC):
                    w = e * FC + fc
                    if w >= 2:
                        ep, fcp = divmod(w - 2, FC)
                        wt(g, "pe", ("hid", ep, fcp, 1))
                    else:
                        wt(g, "pe", ("oproj", 14 + w))
                    src = dr["sw1"][e] if e < 2 else dr["rw1"][e - 2]
                    ins = g.dma_start(
                        out=B.sW1B[:, (w % 2) * DC * 128:(w % 2 + 1) * DC * 128].rearrange(
                            "p (c f) -> p c f", f=128),
                        in_=src[:, fc * 128:(fc + 1) * 128].rearrange("(c p) f -> p c f", p=128),
                    )
                    inc(ins, "ldg", ("w1", e, fc), by=16)
            if e >= 1:
                e2 = e - 1
                for dc in range(DC):
                    w = e2 * DC + dc
                    if w >= 2:
                        ep, dcp = divmod(w - 2, DC)
                        wt(g, "pe", ("w2", ep, dcp, 1))
                    src = dr["sw2"][e2] if e2 < 2 else dr["rw2"][e2 - 2]
                    ins = g.dma_start(
                        out=B.sW2B[:, (w % 2) * FC * 128:(w % 2 + 1) * FC * 128].rearrange(
                            "p (c f) -> p c f", f=128),
                        in_=src[:, dc * 128:(dc + 1) * 128].rearrange("(c p) f -> p c f", p=128),
                    )
                    inc(ins, "ldg", ("w2l", e2, dc), by=16)

    def p2_tensor(t):
        wt(t, "dv", "ctxb")
        for it in range(8):
            for bb in range(4):
                bidx = it * 4 + bb
                if bidx >= 1:
                    wt(t, "dv", ("ctrv", bidx - 1))
                ins = None
                for k in range(4):
                    dc = bb * 4 + k
                    ins = t.transpose(pb[:, k * 128:(k + 1) * 128],
                                      B.sQT[:, it * D + dc * 128: it * D + (dc + 1) * 128],
                                      B.sIDB)
                inc(ins, "pe", ("ctr", bidx))
        wt(t, "dv", ("ctrv", 31))
        for oc2 in range(16):
            wt(t, "ldg", ("wo", oc2))
            wslab = B.sWO[:, (oc2 % 2) * DC * 128:(oc2 % 2 + 1) * DC * 128]
            for tch in range(2):
                bank = (oc2 * 2 + tch) % 4
                if oc2 >= 2:
                    wt(t, "dv", ("opjev", oc2 - 2, tch))
                ins = None
                for dc in range(DC):
                    ins = t.matmul(pp[bank][:, :], wslab[:, dc * 128:(dc + 1) * 128],
                                   B.sCTXT[:, dc * T + tch * 512: dc * T + tch * 512 + 512],
                                   start=(dc == 0), stop=(dc == DC - 1))
                inc(ins, "pe", ("opj", oc2, tch), *((("oproj", oc2),) if tch == 1 else ()))
        wt(t, "dv", ("opjev", 15, 1))
        for tch in range(2):
            for dc in range(DC):
                k = tch * DC + dc
                xs = B.sX2[:, dc * T + tch * 512: dc * T + tch * 512 + 512]
                ins = t.matmul(pp[4 + tch][0:1, :], B.sONESC, xs,
                               start=(dc == 0), stop=(dc == DC - 1))
                wt(t, "ac", ("xsq2", k))
                ins = t.matmul(pp[0 + tch][0:1, :], B.sONESC,
                               B.sXSQ2[:, (k % 2) * 512:(k % 2) * 512 + 512],
                               start=(dc == 0), stop=(dc == DC - 1))
                inc(ins, "pe", ("sqmm2", k))
        wt(t, "dv", "ab2")
        bank_ab2 = {(0, 0): 2, (0, 1): 3, (1, 0): 4, (1, 1): 5}
        ins = None
        for which in range(2):
            for tch in range(2):
                ins = t.matmul(pp[bank_ab2[(which, tch)]][:, :], B.sONESR[0:1, 0:128],
                               B.sSTAT[0:1, 3072 + which * 1024 + tch * 512: 3072 + which * 1024 + (tch + 1) * 512],
                               start=True, stop=True)
        inc(ins, "pe", "bc2", "p2pe")

    def p2_scalar(s):
        wt(s, "dv", "ctxb")
        wt(s, "hds", "xload2")
        for tch in range(2):
            for dc in range(DC):
                k = tch * DC + dc
                if k >= 2:
                    wt(s, "pe", ("sqmm2", k - 2))
                wt(s, "dv", ("opjev", dc, tch))
                ins = s.activation(out=B.sXSQ2[:, (k % 2) * 512:(k % 2) * 512 + 512],
                                   in_=B.sX2[:, dc * T + tch * 512: dc * T + tch * 512 + 512],
                                   func=ACTF.Square)
                inc(ins, "ac", ("xsq2", k))
        wt(s, "dv", "var2")
        ins = s.activation(out=B.sSTAT[0:1, 2048:3072], in_=B.sSTAT[0:1, 4096:5120],
                           func=ACTF.Sqrt)
        inc(ins, "ac", "sqrt2")

    def p2_vector(v):
        for bidx in range(32):
            it, bb = divmod(bidx, 4)
            wt(v, "pe", ("ctr", bidx))
            ins = v.tensor_copy(
                B.sCTXT.rearrange("p (c t) -> p c t", t=T)[:, bb * 4:(bb + 1) * 4, it * 128:(it + 1) * 128],
                pb[:, 0:512].rearrange("p (k q) -> p k q", q=128),
            ) if not dry else None
            inc(ins, "dv", ("ctrv", bidx))
        wt(v, "hds", "xload2")
        for dc in range(DC):
            ins = v.tensor_scalar(B.sX2[:, dc * T:(dc + 1) * T],
                                  B.sX2[:, dc * T:(dc + 1) * T],
                                  B.sGAB[:, dc:dc + 1], None, ALU.add) if not dry else None
            inc(ins, "dv", ("xbias", dc))
        for oc2 in range(16):
            for tch in range(2):
                wt(v, "pe", ("opj", oc2, tch))
                ins = v.scalar_tensor_tensor(
                    B.sX2[:, oc2 * T + tch * 512: oc2 * T + tch * 512 + 512],
                    pp[(oc2 * 2 + tch) % 4][:, :], B.sGA[:, oc2:oc2 + 1],
                    B.sX2[:, oc2 * T + tch * 512: oc2 * T + tch * 512 + 512],
                    ALU.mult, ALU.add) if not dry else None
                inc(ins, "dv", ("opjev", oc2, tch))
        wt(v, "pe", ("sqmm2", 31))
        if not dry:
            v.tensor_scalar(B.sSTAT[0:1, 0:512], pp[4][0:1, :], 1.0 / D, None, ALU.mult)
            v.tensor_scalar(B.sSTAT[0:1, 512:1024], pp[5][0:1, :], 1.0 / D, None, ALU.mult)
            v.tensor_scalar(B.sSTAT[0:1, 3072:3584], pp[0][0:1, :], 1.0 / D, None, ALU.mult)
            v.tensor_scalar(B.sSTAT[0:1, 3584:4096], pp[1][0:1, :], 1.0 / D, None, ALU.mult)
            v.tensor_mul(B.sSTAT[0:1, 4096:5120], B.sSTAT[0:1, 0:1024], B.sSTAT[0:1, 0:1024])
            v.tensor_sub(B.sSTAT[0:1, 1024:2048], B.sSTAT[0:1, 3072:4096], B.sSTAT[0:1, 4096:5120])
            v.tensor_scalar(B.sSTAT[0:1, 1024:2048], B.sSTAT[0:1, 1024:2048], EPS, None, ALU.add)
        ins = v.reciprocal(B.sSTAT[0:1, 4096:5120], B.sSTAT[0:1, 1024:2048]) if not dry else None
        inc(ins, "dv", "var2")
        wt(v, "ac", "sqrt2")
        if not dry:
            r0 = B.sSTAT[0:1, 2048:3072]
            v.tensor_mul(B.sSTAT[0:1, 5120:6144], r0, r0)
            v.tensor_mul(B.sSTAT[0:1, 5120:6144], B.sSTAT[0:1, 5120:6144], B.sSTAT[0:1, 1024:2048])
            v.tensor_scalar(B.sSTAT[0:1, 5120:6144], B.sSTAT[0:1, 5120:6144], -0.5, 1.5, ALU.mult, ALU.add)
            v.tensor_mul(B.sSTAT[0:1, 3072:4096], B.sSTAT[0:1, 5120:6144], r0)
            v.scalar_tensor_tensor(B.sSTAT[0:1, 4096:5120], B.sSTAT[0:1, 0:1024], -1.0,
                                   B.sSTAT[0:1, 3072:4096], ALU.mult, ALU.mult)
        ins = v.tensor_copy(B.sSTAT[0:1, 5120:5121], B.sSTAT[0:1, 3072:3073]) if not dry else None
        inc(ins, "dv", "ab2")
        wt(v, "pe", "bc2")
        if "dbg_x2" in dr:
            wt(v, "hds", "dbgx2")
        bank_ab2 = {(0, 0): 2, (0, 1): 3, (1, 0): 4, (1, 1): 5}
        ins = None
        for dc in range(DC):
            for tch in range(2):
                sl = slice(dc * T + tch * 512, dc * T + tch * 512 + 512)
                if not dry:
                    v.tensor_mul(B.sTMP, B.sX2[:, sl], pp[bank_ab2[(0, tch)]][:, :])
                    v.tensor_add(B.sTMP, B.sTMP, pp[bank_ab2[(1, tch)]][:, :])
                    v.tensor_scalar(B.sH[:, sl], B.sTMP, B.sCB1[:, dc:dc + 1],
                                    B.sMODT[:, 48 + dc:49 + dc], ALU.mult, ALU.add)
                    v.scalar_tensor_tensor(B.sX2[:, sl], B.sH[:, sl], B.sGB[:, dc:dc + 1],
                                           B.sX2[:, sl], ALU.mult, ALU.add)
                ins = v.tensor_scalar(B.sX2[:, sl], B.sX2[:, sl], B.sGBS2[:, dc:dc + 1],
                                      None, ALU.add) if not dry else None
        inc(ins, "dv", "hdone2", "p2dv")

    run_phase({"sync": p2_sync, "gpsimd": p2_gpsimd, "tensor": p2_tensor,
               "scalar": p2_scalar, "vector": p2_vector})
    if not dry:
        st2b.close()

    st3 = ExitStack()
    if not dry:
        st3.__enter__()
    alloc(st3, "sHID", [128, 2 * FC * T], BF16)
    alloc(st3, "sGBC", [128, 2 * T], BF16)
    alloc(st3, "sESEL", [16, 16 * 128], F32)
    alloc(st3, "sTKS", [128, 64], F32)
    alloc(st3, "sSCT", [128, 8 * 16], F32)
    alloc(st3, "sGT", [128, 8 * 16], F32)
    alloc(st3, "sSCO", [16, T], F32)
    if not dry:
        B.sGTT = B.sSCO

    def p3_sync(q):
        q = Chain(q, "hds")
        wt(q, "dv", "p2dv")
        if "dbg_h2" in dr:
            ins = q.dma_start(out=dr["dbg_fst"], in_=B.sMN)
            inc(ins, "hds", by=16)
            ins = q.dma_start(out=dr["dbg_fst2"], in_=B.sLL)
            inc(ins, "hds", by=16)
            ins = q.dma_start(out=dr["dbg_ctx"], in_=B.sQT)
            inc(ins, "hds", by=16)
            ins = q.dma_start(out=dr["dbg_mod"], in_=dr["mod_all"])
            inc(ins, "hds", by=16)
            ins = q.dma_start(out=dr["dbg_kall"], in_=dr["k_all"])
            inc(ins, "hds", by=16)
            ins = q.dma_start(out=dr["dbg_vall"], in_=dr["v_all"])
            inc(ins, "hds", by=16)
            ins = q.dma_start(out=dr["dbg_h2"], in_=B.sH)
            inc(ins, "hds", by=16)
            wt(q, "dv", ("gttv", 7))
            ins = q.dma_start(out=dr["dbg_gtt"], in_=B.sGTT)
            inc(ins, "hds", by=16)
        ins = q.dma_start(out=B.sESEL, in_=dr["esel"])
        inc(ins, "hds", "p3sm", by=16)
        wt(q, "pe", ("gbc", NE - 1))
        ins = q.dma_start(out=B.sESEL[0:16, 0:D], in_=dr["rb2"])
        inc(ins, "hds", "p3rb2", by=16)
        wt(q, "dv", ("b2ev", 15, 1))
        ins = q.dma_start(
            out=dr["y"].rearrange("(c p) t -> p c t", p=128),
            in_=B.sX2.rearrange("p (c t) -> p c t", t=T),
        )
        inc(ins, "hds", "ystore", by=16)
        wtv(q, "hds", S.n.get("hds", 0) if dry else S.at("hds", "ystore"))

    def p3_tensor(t):
        wt(t, "dv", "p2dv")
        for tch in range(2):
            ins = None
            for dc in range(DC):
                ins = t.matmul(pp[4 + tch][0:16, :], B.sCENTB[:, dc * 16:(dc + 1) * 16],
                               B.sH[:, dc * T + tch * 512: dc * T + tch * 512 + 512],
                               start=(dc == 0), stop=(dc == DC - 1))
            inc(ins, "pe", ("scot", tch))
        wt(t, "dv", "scob")
        for tt in range(8):
            if tt >= 1:
                wt(t, "dv", ("sctv", tt - 1))
            ins = t.transpose(pp[6][:, 0:16], B.sSCO[0:16, tt * 128:(tt + 1) * 128],
                              B.sIDF[0:16, 0:16])
            inc(ins, "pe", ("sct", tt))
        for tt in range(8):
            wt(t, "dv", ("top4", tt))
            if tt >= 1:
                wt(t, "dv", ("gttv", tt - 1))
            else:
                wt(t, "dv", ("sctv", 7))
            ins = t.transpose(pp[6][0:16, 0:128], B.sGT[:, tt * 16:(tt + 1) * 16],
                              B.sIDF)
            inc(ins, "pe", ("gtt", tt))
        for e in range(NE + 1):
            if e < NE:
                if e >= 2:
                    j = e - 2
                    wt(t, "dv", ("gttv", 7))
                    if e == 2:
                        wt(t, "ac", ("scos", 1))
                        wt(t, "hds", "p3sm")
                    else:
                        wt(t, "dv", ("gbcv", e - 1))
                    ins = None
                    for tch in range(2):
                        ins = t.matmul(pp[4 + tch][:, :], B.sESEL[0:16, j * 128:(j + 1) * 128],
                                       B.sGTT[0:16, tch * 512:(tch + 1) * 512],
                                       start=True, stop=True)
                    inc(ins, "pe", ("gbc", e))
                for fc in range(FC):
                    wt(t, "ldg", ("w1", e, fc))
                    wslab = B.sW1B[:, ((e * FC + fc) % 2) * DC * 128:((e * FC + fc) % 2 + 1) * DC * 128]
                    for tch in range(2):
                        if e >= 1 or fc >= 1:
                            pfc = fc - 1 if fc >= 1 else FC - 1
                            pe_ = e if fc >= 1 else e - 1
                            wt(t, "ac", ("hrelu", pe_, pfc, tch))
                        ins = None
                        for dc in range(DC):
                            ins = t.matmul(pp[tch][:, :], wslab[:, dc * 128:(dc + 1) * 128],
                                           B.sH[:, dc * T + tch * 512: dc * T + tch * 512 + 512],
                                           start=(dc == 0), stop=(dc == DC - 1))
                        inc(ins, "pe", ("hid", e, fc, tch))
            if e >= 1:
                ep = e - 1
                if ep >= 2:
                    wt(t, "dv", ("hscale", ep, FC - 1))
                else:
                    wt(t, "ac", ("hrelu", ep, FC - 1, 1))
                for dc in range(DC):
                    wt(t, "ldg", ("w2l", ep, dc))
                    wslab = B.sW2B[:, ((ep * DC + dc) % 2) * FC * 128:((ep * DC + dc) % 2 + 1) * FC * 128]
                    for tch in range(2):
                        if ep >= 1 or dc >= 1:
                            pdc = dc - 1 if dc >= 1 else DC - 1
                            pep = ep if dc >= 1 else ep - 1
                            wt(t, "dv", ("w2ev", pep, pdc, tch))
                        ins = None
                        for fc in range(FC):
                            ins = t.matmul(pp[2 + tch][:, :], wslab[:, fc * 128:(fc + 1) * 128],
                                           B.sHID[:, (ep % 2) * FC * T + fc * T + tch * 512:(ep % 2) * FC * T + fc * T + tch * 512 + 512],
                                           start=(fc == 0), stop=(fc == FC - 1))
                        inc(ins, "pe", ("w2", ep, dc, tch))
        wt(t, "dv", ("gbcv", NE - 1))
        wt(t, "hds", "p3rb2")
        for dc in range(DC):
            for tch in range(2):
                if dc >= 1:
                    wt(t, "dv", ("b2ev", dc - 1, tch))
                ins = t.matmul(pp[4 + tch][:, :], B.sESEL[0:16, dc * 128:(dc + 1) * 128],
                               B.sGTT[0:16, tch * 512:(tch + 1) * 512],
                               start=True, stop=True)
                inc(ins, "pe", ("b2", dc, tch))

    def p3_scalar(s):
        wt(s, "pe", "p2pe")
        wt(s, "dv", "p2dv")
        for tch in range(2):
            wt(s, "pe", ("scot", tch))
            ins = s.activation(out=B.sSCO[0:16, tch * 512:(tch + 1) * 512],
                               in_=pp[4 + tch][0:16, :], func=ACTF.Sigmoid)
            inc(ins, "ac", ("scos", tch))
        for e in range(NE):
            for fc in range(FC):
                for tch in range(2):
                    wt(s, "pe", ("hid", e, fc, tch))
                    if e >= 2 and fc == 0 and tch == 0:
                        wt(s, "pe", ("w2", e - 2, DC - 1, 1))
                    if e < 2:
                        bias = B.sSB1[:, e * FC + fc: e * FC + fc + 1]
                    else:
                        bias = B.sRB1[:, (e - 2) * FC + fc: (e - 2) * FC + fc + 1]
                    ins = s.activation(
                        out=B.sHID[:, (e % 2) * FC * T + fc * T + tch * 512:(e % 2) * FC * T + fc * T + tch * 512 + 512],
                        in_=pp[tch][:, :], func=ACTF.Relu, bias=bias)
                    inc(ins, "ac", ("hrelu", e, fc, tch))

    def p3_vector(v):
        wt(v, "pe", "p2pe")
        wt(v, "ac", ("scos", 1))
        ins = v.tensor_scalar(B.sSCO, B.sSCO, B.sFB, None, ALU.add) if not dry else None
        inc(ins, "dv", "scob")
        for tt in range(8):
            wt(v, "pe", ("sct", tt))
            ins = v.tensor_copy(B.sSCT[:, tt * 16:(tt + 1) * 16], pp[6][:, 0:16]) if not dry else None
            inc(ins, "dv", ("sctv", tt))
        for tt in range(8):
            ssl = B.sSCT[:, tt * 16:(tt + 1) * 16]
            gsl = B.sGT[:, tt * 16:(tt + 1) * 16]
            if not dry:
                w = B.sTKS[:, 0:16]
                mcol = B.sTKS[:, 16:17]
                ge = B.sTKS[:, 32:48]
            ins = v.tensor_copy(w, ssl) if not dry else None
            drain(v, ins)
            for rnd in range(3):
                ins = v.reduce_max(mcol, w, axis=AXX) if not dry else None
                drain(v, ins)
                ins = v.tensor_scalar(ge, w, mcol, None, ALU.is_ge) if not dry else None
                drain(v, ins)
                ins = v.scalar_tensor_tensor(w, ge, BIGNEG, w, ALU.mult, ALU.add) if not dry else None
                drain(v, ins)
            ins = v.reduce_max(mcol, w, axis=AXX) if not dry else None
            drain(v, ins)
            ins = v.tensor_scalar(ge, ssl, mcol, None, ALU.is_ge) if not dry else None
            drain(v, ins)
            ins = v.tensor_mul(gsl, ge, ssl) if not dry else None
            drain(v, ins)
            ins = v.reduce_sum(mcol, gsl, axis=AXX) if not dry else None
            drain(v, ins)
            ins = v.tensor_scalar_max(mcol, mcol, 1e-8) if not dry else None
            drain(v, ins)
            ins = v.reciprocal(B.sTKS[:, 17:18], mcol) if not dry else None
            drain(v, ins)
            ins = v.tensor_scalar(gsl, gsl, B.sTKS[:, 17:18], None, ALU.mult) if not dry else None
            inc(ins, "dv", ("top4", tt))
        for tt in range(8):
            wt(v, "pe", ("gtt", tt))
            ins = v.tensor_copy(B.sGTT[0:16, tt * 128:(tt + 1) * 128], pp[6][0:16, 0:128]) if not dry else None
            inc(ins, "dv", ("gttv", tt))
        for e in range(NE + 1):
            if e < NE and e >= 2:
                wt(v, "pe", ("gbc", e))
                ins = None
                for tch in range(2):
                    ins = v.tensor_copy(B.sGBC[:, (e % 2) * T + tch * 512:(e % 2) * T + tch * 512 + 512],
                                        pp[4 + tch][:, :]) if not dry else None
                inc(ins, "dv", ("gbcv", e))
                for fc in range(FC):
                    wt(v, "ac", ("hrelu", e, fc, 1))
                    hsl = B.sHID[:, (e % 2) * FC * T + fc * T:(e % 2) * FC * T + (fc + 1) * T]
                    ins = v.tensor_mul(hsl, hsl, B.sGBC[:, (e % 2) * T:(e % 2) * T + T]) if not dry else None
                    inc(ins, "dv", ("hscale", e, fc))
            elif e < NE:
                inc(None, "dv", ("gbcv", e), by=0)
            if e >= 1:
                ep = e - 1
                for dc in range(DC):
                    for tch in range(2):
                        wt(v, "pe", ("w2", ep, dc, tch))
                        sl = slice(dc * T + tch * 512, dc * T + tch * 512 + 512)
                        ins = v.scalar_tensor_tensor(
                            B.sX2[:, sl], pp[2 + tch][:, :], B.sGB[:, dc:dc + 1],
                            B.sX2[:, sl], ALU.mult, ALU.add) if not dry else None
                        inc(ins, "dv", ("w2ev", ep, dc, tch))
        for dc in range(DC):
            for tch in range(2):
                wt(v, "pe", ("b2", dc, tch))
                sl = slice(dc * T + tch * 512, dc * T + tch * 512 + 512)
                ins = v.scalar_tensor_tensor(
                    B.sX2[:, sl], pp[4 + tch][:, :], B.sGB[:, dc:dc + 1],
                    B.sX2[:, sl], ALU.mult, ALU.add) if not dry else None
                inc(ins, "dv", ("b2ev", dc, tch))

    run_phase({"sync": p3_sync, "tensor": p3_tensor, "scalar": p3_scalar,
               "vector": p3_vector})
    if not dry:
        st3.close()
        st2a.close()

    if not S.frozen:
        S.c[("pe", "p0pe")] = S.c[("pe", ("qkv", OQ - 1))]
        S.c[("dv", "p0dv")] = S.c[("dv", ("kev", 15, 1))]
        S.c[("ac", "p0ac")] = S.c[("ac", ("vevB", 47))]


def build_nc():
    nc = bass.Bass("TRN2", target_bir_lowering=False, debug=False,
                   detect_race_conditions=False)
    dr = {}

    def di(name, shape, dt=F32):
        dr[name] = nc.dram_tensor(name, shape, dt, kind="ExternalInput").ap()

    di("xT", [D, T])
    di("wqkvT", [D, 3 * D])
    di("woutT", [D, D])
    di("centT", [D, 16])
    di("c2", [128, 16])
    di("bqkv2", [128, OQ])
    di("bqvr", [1, D])
    di("bout2", [128, 16])
    di("sb1_2", [128, 22])
    di("rb1_2", [128, 176])
    di("sb2_2", [128, 32])
    di("rb2", [16, D])
    di("fbias", [16, 1])
    di("ones", [1, 128])
    di("ones_col", [128, 1])
    di("ident", [128, 128])
    di("esel", [16, 16 * 128])
    di("wmod_sl", [D, MODSL])
    di("sw1", [2, D, DFF])
    di("sw2", [2, DFF, D])
    di("rw1", [16, D, DFF])
    di("rw2", [16, DFF, D])
    dr["y"] = nc.dram_tensor("y", [D, T], F32, kind="ExternalOutput").ap()
    import os as _os
    if _os.environ.get("KDBG"):
        dr["dbg_mod"] = nc.dram_tensor("dbg_mod", [R, MODSL], F32, kind="ExternalOutput").ap()
        dr["dbg_kall"] = nc.dram_tensor("dbg_kall", [R * D, T], BF16, kind="ExternalOutput").ap()
        dr["dbg_vall"] = nc.dram_tensor("dbg_vall", [R * T, D], BF16, kind="ExternalOutput").ap()
        dr["dbg_h2"] = nc.dram_tensor("dbg_h2", [128, DC * T], BF16, kind="ExternalOutput").ap()
        dr["dbg_gtt"] = nc.dram_tensor("dbg_gtt", [16, T], F32, kind="ExternalOutput").ap()
        dr["dbg_x2"] = nc.dram_tensor("dbg_x2", [128, DC * T], F32, kind="ExternalOutput").ap()
        dr["dbg_q"] = nc.dram_tensor("dbg_q", [128, DC * T], BF16, kind="ExternalOutput").ap()
        dr["dbg_ctx"] = nc.dram_tensor("dbg_ctx", [128, DC * T], BF16, kind="ExternalOutput").ap()
        dr["dbg_kb"] = nc.dram_tensor("dbg_kb", [128, DC * 512], BF16, kind="ExternalOutput").ap()
        dr["dbg_vb"] = nc.dram_tensor("dbg_vb", [128, 4 * D], BF16, kind="ExternalOutput").ap()
        dr["dbg_sc"] = nc.dram_tensor("dbg_sc", [128, 512], F32, kind="ExternalOutput").ap()
        dr["dbg_pr"] = nc.dram_tensor("dbg_pr", [128, 512], BF16, kind="ExternalOutput").ap()
        for i in range(7):
            dr[f"dbg_st{i}"] = nc.dram_tensor(f"dbg_st{i}", [128, 1], F32, kind="ExternalOutput").ap()
        dr["dbg_st7"] = nc.dram_tensor("dbg_st7", [128, 512], BF16, kind="ExternalOutput").ap()
        dr["dbg_fst"] = nc.dram_tensor("dbg_fst", [128, 8], F32, kind="ExternalOutput").ap()
        dr["dbg_fst2"] = nc.dram_tensor("dbg_fst2", [128, 8], F32, kind="ExternalOutput").ap()
    dr["mod_in"] = nc.dram_tensor("mod_in", [1, MODSL], F32, kind="Internal").ap()
    dr["mod_all"] = nc.dram_tensor("mod_all", [R, MODSL], F32, kind="Internal").ap()
    dr["k_in"] = nc.dram_tensor("k_in", [D, T], BF16, kind="Internal").ap()
    dr["k_all"] = nc.dram_tensor("k_all", [R * D, T], BF16, kind="Internal").ap()
    dr["v_in"] = nc.dram_tensor("v_in", [T, D], BF16, kind="Internal").ap()
    dr["v_all"] = nc.dram_tensor("v_all", [R * T, D], BF16, kind="Internal").ap()

    with ExitStack() as ctx:
        sems = {n: ctx.enter_context(nc.semaphore(n))
                for n in ("pe", "dv", "ac", "hds", "ldg", "cc1", "cc2", "cc3")}
        S = Sched()
        emit(nc, ctx, dr, sems, S, dry=True)
        S.frozen = True
        with nc.Block() as blk:
            emit(nc, ctx, dr, sems, S, dry=False, blk=blk)
    return nc


def kernel(L0_attn_hidden_states, L0_attn_w_qkv, L0_attn_b_qkv, L0_attn_w_out,
           L0_attn_b_out, L0_ffn_shared_w1, L0_ffn_shared_b1, L0_ffn_shared_w2,
           L0_ffn_shared_b2, L0_ffn_routed_w1, L0_ffn_routed_b1, L0_ffn_routed_w2,
           L0_ffn_routed_b2, L0_ffn_centroids, L0_ffn_bias, L0_c, L0_wmod):
    f = np.float32
    x = np.asarray(L0_attn_hidden_states, f)
    common = {
        "wqkvT": np.ascontiguousarray(np.asarray(L0_attn_w_qkv, f).T),
        "woutT": np.ascontiguousarray(np.asarray(L0_attn_w_out, f).T),
        "centT": np.ascontiguousarray(np.asarray(L0_ffn_centroids, f).T),
        "c2": np.ascontiguousarray(np.asarray(L0_c, f).reshape(16, 128).T),
        "bqkv2": np.ascontiguousarray(np.asarray(L0_attn_b_qkv, f).reshape(OQ, 128).T),
        "bqvr": np.ascontiguousarray(np.asarray(L0_attn_b_qkv, f)[4096:6144].reshape(1, D)),
        "bout2": np.ascontiguousarray(np.asarray(L0_attn_b_out, f).reshape(16, 128).T),
        "sb1_2": np.ascontiguousarray(
            np.asarray(L0_ffn_shared_b1, f).reshape(2, FC, 128).transpose(2, 0, 1).reshape(128, 22)),
        "rb1_2": np.ascontiguousarray(
            np.asarray(L0_ffn_routed_b1, f).reshape(16, FC, 128).transpose(2, 0, 1).reshape(128, 176)),
        "sb2_2": np.ascontiguousarray(
            np.asarray(L0_ffn_shared_b2, f).reshape(2, 16, 128).transpose(2, 0, 1).reshape(128, 32)),
        "rb2": np.ascontiguousarray(np.asarray(L0_ffn_routed_b2, f)),
        "fbias": np.ascontiguousarray(np.asarray(L0_ffn_bias, f).reshape(16, 1)),
        "ones": np.ones((1, 128), f),
        "ones_col": np.ones((128, 1), f),
        "ident": np.eye(128, dtype=f),
        "esel": np.ascontiguousarray(
            np.eye(16, dtype=f)[:, :, None].repeat(128, axis=2).reshape(16, 16 * 128)),
        "sw1": np.ascontiguousarray(np.asarray(L0_ffn_shared_w1, f)),
        "sw2": np.ascontiguousarray(np.asarray(L0_ffn_shared_w2, f)),
        "rw1": np.ascontiguousarray(np.asarray(L0_ffn_routed_w1, f)),
        "rw2": np.ascontiguousarray(np.asarray(L0_ffn_routed_w2, f)),
    }
    wmod = np.asarray(L0_wmod, f)
    in_maps = []
    for r in range(R):
        m = dict(common)
        m["xT"] = np.ascontiguousarray(x[0, r * T:(r + 1) * T, :].T)
        m["wmod_sl"] = np.ascontiguousarray(wmod[:, r * MODSL:(r + 1) * MODSL])
        in_maps.append(m)

    nc = build_nc()
    res = run_bass_kernel_spmd(nc, in_maps, core_ids=list(range(R)))
    out = np.concatenate([np.asarray(res.results[r]["y"]).T for r in range(R)], axis=0)
    return np.ascontiguousarray(out.reshape(1, SG, D), f)


# version 06
# speedup vs baseline: 1.1468x; 1.1468x over previous
import numpy as np
from contextlib import ExitStack
from types import SimpleNamespace

import concourse.bass as bass
import concourse.mybir as mybir
from concourse.bass_utils import run_bass_kernel_spmd

F32 = mybir.dt.float32
BF16 = mybir.dt.bfloat16
AXX = mybir.AxisListType.X
ALU = mybir.AluOpType
ACTF = mybir.ActivationFunctionType

R = 8
T = 1024
D = 2048
DC = 16
OQ = 48
DFF = 1408
FC = 11
NE = 18
SG = 8192
NJC = 16
NIT = 8
NM = NJC * NIT
MODW = 12288
MODSL = MODW // R
SCALE = float(1.0 / np.sqrt(2048.0))
EPS = 1e-5
BIGNEG = -1.0e30


class Sched:
    def __init__(self):
        self.c = {}
        self.n = {}
        self.frozen = False

    def reset_counts(self):
        self.n = {}

    def inc(self, sem, labels=(), by=1):
        v = self.n.get(sem, 0) + by
        self.n[sem] = v
        for L in labels:
            key = (sem, L)
            if self.frozen:
                assert self.c.get(key) == v, f"sched mismatch {key}: {self.c.get(key)} vs {v}"
            else:
                assert key not in self.c, f"dup label {key}"
                self.c[key] = v
        return v

    def at(self, sem, label):
        return self.c[(sem, label)]


class _DryInst:
    def then_inc(self, *a, **k):
        return self


_DRY_INST = _DryInst()


class _DryEng:
    def __getattr__(self, n):
        if n == "wait_ge":
            return lambda *a, **k: None
        return lambda *a, **k: _DRY_INST


class _DryAP:
    def __getitem__(self, *_):
        return self

    def rearrange(self, *a, **k):
        return self

    def opt(self):
        return self


_DRY_AP = _DryAP()


class _DryB:
    def __getattr__(self, n):
        return _DRY_AP


def emit(nc, ctx, dr, sems, S, dry, blk=None):
    S.reset_counts()

    if dry:
        B = _DryB()
    else:
        B = SimpleNamespace()

    def alloc(st, name, shape, dt, attr=None):
        if not dry:
            setattr(B, attr or name, st.enter_context(nc.sbuf_tensor(name, shape, dt))[:])

    def palloc(st, name, shape, dt):
        if not dry:
            setattr(B, name, st.enter_context(nc.psum_tensor(name, shape, dt))[:])

    def inc(ins, sem, *labels, by=1):
        S.inc(sem, labels, by)
        if not dry and ins is not None:
            ins.then_inc(sems[sem], by)

    def wt(eng, sem, label):
        if dry:
            return
        eng.wait_ge(sems[sem], S.at(sem, label))

    def wtv(eng, sem, v):
        if dry:
            return
        eng.wait_ge(sems[sem], v)

    def drain(eng, ins):
        inc(ins, "dv")
        wtv(eng, "dv", S.n["dv"])

    class Chain:

        def __init__(self, q, semname):
            self._q = q
            self._sem = semname

        def dma_start(self, **kw):
            prev = S.n.get(self._sem, 0)
            if prev and not dry:
                self._q.wait_ge(sems[self._sem], prev)
            return self._q.dma_start(**kw)

        def dma_lane(self, semname, **kw):
            prev = S.n.get(semname, 0)
            if prev and not dry:
                self._q.wait_ge(sems[semname], prev)
            return self._q.dma_start(**kw)

        def __getattr__(self, n):
            return getattr(self._q, n)

    FFN_LANE = {}
    _si = 0
    for _e in range(NE + 1):
        if _e < NE:
            for _fc in range(FC):
                FFN_LANE[("w1", _e, _fc)] = ("ldg", "lgb")[_si % 2]
                _si += 1
        if _e >= 1:
            for _dc in range(DC):
                FFN_LANE[("w2l", _e - 1, _dc)] = ("ldg", "lgb")[_si % 2]
                _si += 1

    def run_phase(fns):
        order = ("sync", "gpsimd", "tensor", "scalar", "vector")
        for name in order:
            if name in fns:
                if dry:
                    fns[name](_DryEng())
                else:
                    getattr(blk, name)(fns[name])

    if not dry:
        per = ctx
    alloc(ctx, "sH", [128, DC * T], BF16)
    alloc(ctx, "sQT", [128, DC * T], BF16)
    alloc(ctx, "sMODT", [128, 96], F32)

    def alloc_pack(st, name, cols, views, dt):
        if not dry:
            h = st.enter_context(nc.sbuf_tensor(name, [128, cols], dt))[:]
            for vn, c0, c1 in views:
                setattr(B, vn, h[:, c0:c1])

    alloc_pack(ctx, "sSM16", 144, [
        ("sCA1", 0, 16), ("sCB1", 16, 32), ("sGA", 32, 48), ("sGB", 48, 64),
        ("sGAB", 64, 80), ("sGBS2", 80, 96), ("sSB2S", 96, 112),
        ("sT16a", 112, 128), ("sT16b", 128, 144)], F32)
    alloc_pack(ctx, "sPKB", 310, [
        ("sBQKV", 0, 48), ("sBOUT", 48, 64), ("sSB1", 64, 86),
        ("sRB1", 86, 262), ("sSB22", 262, 294), ("sC2F", 294, 310)], F32)
    alloc_pack(ctx, "sFST", 56, [
        ("sMN", 0, 8), ("sMJ", 8, 16), ("sMD", 16, 24), ("sAL", 24, 32),
        ("sLL", 32, 40), ("sRS", 40, 48), ("sRL", 48, 56)], F32)
    alloc(ctx, "sONESR", [1, 128], F32)
    alloc(ctx, "sONESC", [128, 1], F32)
    alloc(ctx, "sONESCB", [128, 1], BF16)
    alloc(ctx, "sIDF", [128, 128], F32)
    alloc(ctx, "sIDB", [128, 128], BF16)
    alloc(ctx, "sFB", [16, 1], F32)
    alloc(ctx, "sCENTB", [128, DC * 16], BF16)
    alloc(ctx, "sSCB", [128, 16], F32)
    pp = []
    for i in range(7):
        if not dry:
            pp.append(ctx.enter_context(nc.psum_tensor(f"pp{i}", [128, 512], F32))[:])
        else:
            pp.append(_DRY_AP)
    if not dry:
        pb = ctx.enter_context(nc.psum_tensor("pb", [128, 1024], BF16))[:]
    else:
        pb = _DRY_AP

    y_ap = dr["y"]

    st0 = ExitStack()
    if not dry:
        st0.__enter__()
    alloc(st0, "sX", [128, DC * T], BF16)
    alloc(st0, "sWM", [128, 2 * MODSL], F32)
    alloc(st0, "sXSQ", [128, 2 * 512], F32)
    alloc(st0, "sKST", [128, 2 * T], BF16)
    alloc(st0, "sVAS", [128, 8 * D], BF16)
    alloc(st0, "sWQ", [128, 2 * DC * 128], BF16)
    alloc(st0, "sMROW", [1, MODSL], F32)
    alloc(st0, "sMLD", [96, 128], F32)
    alloc(st0, "sBQR", [1, 2048], F32)
    alloc(st0, "sSTAT", [1, 6144], F32)
    alloc(st0, "sTMP0", [128, 512], F32, attr="sTMP")

    def p0_sync(q):
        q = Chain(q, "hds")
        smalls = (
            (B.sONESR, dr["ones"]), (B.sONESC, dr["ones_col"]), (B.sIDF, dr["ident"]),
            (B.sC2F, dr["c2"]), (B.sBQKV, dr["bqkv2"]), (B.sBOUT, dr["bout2"]),
            (B.sSB1, dr["sb1_2"]), (B.sRB1, dr["rb1_2"]), (B.sSB22, dr["sb2_2"]),
            (B.sFB, dr["fbias"]), (B.sBQR, dr["bqvr"]),
        )
        for i, (out, src) in enumerate(smalls):
            ins = q.dma_start(out=out, in_=src)
            inc(ins, "hds", *(("csmall",) if i == len(smalls) - 1 else ()), by=16)
        wt(q, "dv", "modrow")
        ins = q.dma_start(out=dr["mod_in"], in_=B.sMROW)
        inc(ins, "hds", "modin", by=16)
        wtv(q, "cc1", 1)
        ins = q.dma_start(
            out=B.sMLD,
            in_=dr["mod_all"].rearrange("r (c p) -> (r c) p", p=128),
        )
        inc(ins, "hds", "modall", by=16)
        for kc in range(16):
            wt(q, "dv", ("kev", kc, 1))
            ins = q.dma_start(out=dr["k_in"][kc * 128:(kc + 1) * 128, :],
                              in_=B.sKST[:, (kc % 2) * T:(kc % 2) * T + T])
            inc(ins, "hds", ("kin", kc), by=16)
        wt(q, "ac", ("vevB", 47))
        for it in range(8):
            ins = q.dma_start(out=dr["v_in"][it * 128:(it + 1) * 128, :],
                              in_=B.sVAS[:, it * D:(it + 1) * D])
            inc(ins, "hds", ("vin", it), by=16)

    def p0_gpsimd(g):
        g = Chain(g, "ldg")
        ins = g.dma_start(
            out=B.sX.rearrange("p (c t) -> p c t", t=T),
            in_=dr["xT"].rearrange("(c p) t -> p c t", p=128),
        )
        inc(ins, "ldg", "xload", by=16)
        ins = g.dma_start(out=B.sONESCB, in_=dr["ones_col"])
        inc(ins, "ldg", "onescb", by=16)
        ins = g.dma_start(out=B.sIDB, in_=dr["ident"])
        inc(ins, "ldg", "idb", by=16)
        ins = g.dma_start(
            out=B.sCENTB.rearrange("p (c e) -> p c e", e=16),
            in_=dr["centT"].rearrange("(c p) e -> p c e", p=128),
        )
        inc(ins, "ldg", "cent", by=16)
        for kc in range(16):
            if kc >= 2:
                wt(g, "pe", ("mod", kc - 2))
            ins = g.dma_start(out=B.sWM[:, (kc % 2) * MODSL:(kc % 2 + 1) * MODSL],
                              in_=dr["wmod_sl"][kc * 128:(kc + 1) * 128, :])
            inc(ins, "ldg", ("wm", kc), by=16)
        wt(g, "hds", "modin")
        ins = g.collective_compute(
            "AllGather", ALU.bypass, replica_groups=[list(range(R))],
            ins=[dr["mod_in"].opt()], outs=[dr["mod_all"].opt()],
        )
        inc(ins, "cc1", "modag")
        for oc in range(OQ):
            if oc >= 2:
                wt(g, "pe", ("qkv", oc - 2))
            ins = g.dma_start(
                out=B.sWQ[:, (oc % 2) * DC * 128:(oc % 2 + 1) * DC * 128].rearrange(
                    "p (c f) -> p c f", f=128),
                in_=dr["wqkvT"][:, oc * 128:(oc + 1) * 128].rearrange(
                    "(c p) f -> p c f", p=128),
            )
            inc(ins, "ldg", ("wq", oc), by=16)
        wt(g, "hds", ("kin", 15))
        wt(g, "hds", ("vin", 7))
        ins = g.collective_compute(
            "AllGather", ALU.bypass, replica_groups=[list(range(R))],
            ins=[dr["k_in"].opt()], outs=[dr["k_all"].opt()],
        )
        inc(ins, "cc2", "kag")
        ins = g.collective_compute(
            "AllGather", ALU.bypass, replica_groups=[list(range(R))],
            ins=[dr["v_in"].opt()], outs=[dr["v_all"].opt()],
        )
        inc(ins, "cc3", "vag")

    def p0_tensor(t):
        wt(t, "ac", "silu")
        for kc in range(16):
            wt(t, "ldg", ("wm", kc))
            ins = None
            for os in range(3):
                ins = t.matmul(pp[os][0:1, :], B.sSCB[:, kc:kc + 1],
                               B.sWM[:, (kc % 2) * MODSL + os * 512:(kc % 2) * MODSL + (os + 1) * 512],
                               start=(kc == 0), stop=(kc == 15))
            inc(ins, "pe", ("mod", kc))
        wt(t, "hds", "modall")
        ins = t.transpose(pp[6][:, 0:96], B.sMLD, B.sIDF[0:96, 0:96])
        inc(ins, "pe", "modT")
        wt(t, "ldg", "onescb")
        wt(t, "dv", "modtev")
        for tch in range(2):
            for dc in range(DC):
                k = tch * DC + dc
                xs = B.sX[:, dc * T + tch * 512: dc * T + tch * 512 + 512]
                ins = t.matmul(pp[3 + tch][0:1, :], B.sONESCB, xs,
                               start=(dc == 0), stop=(dc == DC - 1))
                wt(t, "ac", ("xsq1", k))
                ins = t.matmul(pp[5 + tch][0:1, :], B.sONESC,
                               B.sXSQ[:, (k % 2) * 512:(k % 2) * 512 + 512],
                               start=(dc == 0), stop=(dc == DC - 1))
                inc(ins, "pe", ("sqmm1", k))
        inc(None, "pe", "stats1", by=0) if False else None
        wt(t, "dv", "ab1")
        wt(t, "dv", "modev")
        bank_ab1 = {(0, 0): 0, (0, 1): 1, (1, 0): 2, (1, 1): 5}
        ins = None
        for which in range(2):
            for tch in range(2):
                ins = t.matmul(pp[bank_ab1[(which, tch)]][:, :], B.sONESR[0:1, 0:128],
                               B.sSTAT[0:1, 3072 + which * 1024 + tch * 512: 3072 + which * 1024 + (tch + 1) * 512],
                               start=True, stop=True)
        inc(ins, "pe", "bc1")
        wt(t, "dv", "hdone1")
        for oc in range(OQ):
            wt(t, "ldg", ("wq", oc))
            wslab = B.sWQ[:, (oc % 2) * DC * 128:(oc % 2 + 1) * DC * 128]
            if oc < 32:
                for tch in range(2):
                    bank = (oc * 2 + tch) % 4
                    if oc >= 2:
                        if oc < 16:
                            wt(t, "dv", ("qev", oc - 2, tch))
                        elif oc == 16 or oc == 17:
                            wt(t, "dv", ("qev", oc - 2, tch))
                        else:
                            wt(t, "dv", ("kev", oc - 2 - 16, tch))
                    ins = None
                    for dc in range(DC):
                        ins = t.matmul(pp[bank][:, :], wslab[:, dc * 128:(dc + 1) * 128],
                                       B.sH[:, dc * T + tch * 512: dc * T + tch * 512 + 512],
                                       start=(dc == 0), stop=(dc == DC - 1))
                    inc(ins, "pe", ("qk", oc, tch), *((("qkv", oc),) if tch == 1 else ()))
            else:
                for it in range(8):
                    half = 4 if it < 4 else 5
                    qtr = it % 4
                    if it == 0:
                        wt(t, "ac", ("vevA", oc - 1)) if oc > 32 else wt(t, "dv", "hdone1")
                    if it == 4:
                        wt(t, "ac", ("vevB", oc - 1)) if oc > 32 else None
                        if oc == 32:
                            wt(t, "dv", "hdone1")
                    ins = None
                    for dc in range(DC):
                        ins = t.matmul(pp[half][:, qtr * 128:(qtr + 1) * 128],
                                       B.sH[:, dc * T + it * 128: dc * T + it * 128 + 128],
                                       wslab[:, dc * 128:(dc + 1) * 128],
                                       start=(dc == 0), stop=False)
                    ins = t.matmul(pp[half][:, qtr * 128:(qtr + 1) * 128],
                                   B.sONESR[0:1, 0:128],
                                   B.sBQR[0:1, oc * 128 - 4096:(oc + 1) * 128 - 4096],
                                   start=False, stop=True)
                    if it == 3:
                        inc(ins, "pe", ("qkA", oc))
                    elif it == 7:
                        inc(ins, "pe", ("qkB", oc), ("qkv", oc))
                    else:
                        inc(ins, "pe")
        inc(None, "pe", by=0) if False else None

    def p0_scalar(s):
        wt(s, "hds", "csmall")
        ins = s.activation(out=B.sSCB, in_=B.sC2F, func=ACTF.Silu)
        inc(ins, "ac", "silu")
        wt(s, "ldg", "xload")
        for tch in range(2):
            for dc in range(DC):
                k = tch * DC + dc
                if k >= 2:
                    wt(s, "pe", ("sqmm1", k - 2))
                ins = s.activation(out=B.sXSQ[:, (k % 2) * 512:(k % 2) * 512 + 512],
                                   in_=B.sX[:, dc * T + tch * 512: dc * T + tch * 512 + 512],
                                   func=ACTF.Square)
                inc(ins, "ac", ("xsq1", k))
        wt(s, "dv", "var1")
        ins = s.activation(out=B.sSTAT[0:1, 2048:3072], in_=B.sSTAT[0:1, 4096:5120],
                           func=ACTF.Sqrt)
        inc(ins, "ac", "sqrt1")
        for oc in range(32, OQ):
            wt(s, "pe", ("qkA", oc))
            if oc >= 34:
                wt(s, "hds", ("vin", 7)) if False else None
            ins = s.activation(
                out=B.sVAS.rearrange("p (a d) -> p a d", d=D)[:, 0:4, oc * 128 - 4096: oc * 128 - 4096 + 128],
                in_=pp[4].rearrange("p (a q) -> p a q", q=128),
                func=ACTF.Copy)
            inc(ins, "ac", ("vevA", oc))
            wt(s, "pe", ("qkB", oc))
            ins = s.activation(
                out=B.sVAS.rearrange("p (a d) -> p a d", d=D)[:, 4:8, oc * 128 - 4096: oc * 128 - 4096 + 128],
                in_=pp[5].rearrange("p (a q) -> p a q", q=128),
                func=ACTF.Copy)
            inc(ins, "ac", ("vevB", oc))

    def p0_vector(v):
        wt(v, "pe", ("mod", 15))
        ins = None
        for os in range(3):
            ins = v.tensor_copy(B.sMROW[0:1, os * 512:(os + 1) * 512], pp[os][0:1, :])
        inc(ins, "dv", "modrow", "modev")
        wt(v, "pe", "modT")
        ins = v.tensor_copy(B.sMODT, pp[6][:, 0:96])
        inc(ins, "dv", "modtev")
        wt(v, "hds", "csmall")
        v.tensor_scalar(B.sCA1, B.sMODT[:, 16:32], 1.0, None, ALU.add) if not dry else None
        v.tensor_scalar(B.sCB1, B.sMODT[:, 64:80], 1.0, None, ALU.add) if not dry else None
        for (src0, dstname) in ((32, "sGA"), (80, "sGB")):
            gsl = B.sMODT[:, src0:src0 + 16]
            dst = getattr(B, dstname) if not dry else None
            ins = v.tensor_single_scalar(B.sT16a, gsl, 0.0, ALU.is_ge) if not dry else None
            drain(v, ins)
            ins = v.tensor_scalar(B.sT16a, B.sT16a, 2.0, -1.0, ALU.mult, ALU.add) if not dry else None
            drain(v, ins)
            ins = v.tensor_scalar(B.sT16b, gsl, -1.0, None, ALU.mult) if not dry else None
            drain(v, ins)
            ins = v.tensor_max(B.sT16b, B.sT16b, gsl) if not dry else None
            drain(v, ins)
            ins = v.tensor_scalar_max(B.sT16b, B.sT16b, 0.25) if not dry else None
            drain(v, ins)
            ins = v.tensor_mul(dst, B.sT16a, B.sT16b) if not dry else None
            drain(v, ins)
        ins = v.tensor_mul(B.sGAB, B.sGA, B.sBOUT) if not dry else None
        drain(v, ins)
        ins = v.tensor_add(B.sSB2S, B.sSB22[:, 0:16], B.sSB22[:, 16:32]) if not dry else None
        drain(v, ins)
        ins = v.tensor_mul(B.sGBS2, B.sGB, B.sSB2S) if not dry else None
        inc(ins, "dv", "modderiv")
        wt(v, "pe", ("sqmm1", 31))
        if not dry:
            v.tensor_scalar(B.sSTAT[0:1, 0:512], pp[3][0:1, :], 1.0 / D, None, ALU.mult)
            v.tensor_scalar(B.sSTAT[0:1, 512:1024], pp[4][0:1, :], 1.0 / D, None, ALU.mult)
            v.tensor_scalar(B.sSTAT[0:1, 3072:3584], pp[5][0:1, :], 1.0 / D, None, ALU.mult)
            v.tensor_scalar(B.sSTAT[0:1, 3584:4096], pp[6][0:1, :], 1.0 / D, None, ALU.mult)
            v.tensor_mul(B.sSTAT[0:1, 4096:5120], B.sSTAT[0:1, 0:1024], B.sSTAT[0:1, 0:1024])
            v.tensor_sub(B.sSTAT[0:1, 1024:2048], B.sSTAT[0:1, 3072:4096], B.sSTAT[0:1, 4096:5120])
            v.tensor_scalar(B.sSTAT[0:1, 1024:2048], B.sSTAT[0:1, 1024:2048], EPS, None, ALU.add)
        ins = v.reciprocal(B.sSTAT[0:1, 4096:5120], B.sSTAT[0:1, 1024:2048]) if not dry else None
        inc(ins, "dv", "var1")
        wt(v, "ac", "sqrt1")
        if not dry:
            r0 = B.sSTAT[0:1, 2048:3072]
            v.tensor_mul(B.sSTAT[0:1, 5120:6144], r0, r0)
            v.tensor_mul(B.sSTAT[0:1, 5120:6144], B.sSTAT[0:1, 5120:6144], B.sSTAT[0:1, 1024:2048])
            v.tensor_scalar(B.sSTAT[0:1, 5120:6144], B.sSTAT[0:1, 5120:6144], -0.5, 1.5, ALU.mult, ALU.add)
            v.tensor_mul(B.sSTAT[0:1, 3072:4096], B.sSTAT[0:1, 5120:6144], r0)
            v.scalar_tensor_tensor(B.sSTAT[0:1, 4096:5120], B.sSTAT[0:1, 0:1024], -1.0,
                                   B.sSTAT[0:1, 3072:4096], ALU.mult, ALU.mult)
        ins = v.tensor_copy(B.sSTAT[0:1, 5120:5121], B.sSTAT[0:1, 3072:3073]) if not dry else None
        inc(ins, "dv", "ab1")
        wt(v, "pe", "bc1")
        bank_ab1 = {(0, 0): 0, (0, 1): 1, (1, 0): 2, (1, 1): 5}
        ins = None
        for dc in range(DC):
            for tch in range(2):
                xs = B.sX[:, dc * T + tch * 512: dc * T + tch * 512 + 512]
                if not dry:
                    v.tensor_mul(B.sTMP, xs, pp[bank_ab1[(0, tch)]][:, :])
                    v.tensor_add(B.sTMP, B.sTMP, pp[bank_ab1[(1, tch)]][:, :])
                ins = v.tensor_scalar(B.sH[:, dc * T + tch * 512: dc * T + tch * 512 + 512],
                                      B.sTMP, B.sCA1[:, dc:dc + 1],
                                      B.sMODT[:, dc:dc + 1], ALU.mult, ALU.add) if not dry else None
        inc(ins, "dv", "hdone1")
        for oc in range(32):
            for tch in range(2):
                wt(v, "pe", ("qk", oc, tch))
                bank = (oc * 2 + tch) % 4
                if oc < 16:
                    ins = v.tensor_scalar(B.sQT[:, oc * T + tch * 512: oc * T + tch * 512 + 512],
                                          pp[bank][:, :], B.sBQKV[:, oc:oc + 1], SCALE,
                                          ALU.add, ALU.mult) if not dry else None
                    inc(ins, "dv", ("qev", oc, tch))
                else:
                    kc = oc - 16
                    if kc >= 2:
                        wt(v, "hds", ("kin", kc - 2))
                    ins = v.tensor_scalar(B.sKST[:, (kc % 2) * T + tch * 512:(kc % 2) * T + tch * 512 + 512],
                                          pp[bank][:, :], B.sBQKV[:, oc:oc + 1], None,
                                          ALU.add) if not dry else None
                    inc(ins, "dv", ("kev", kc, tch))

    run_phase({"sync": p0_sync, "gpsimd": p0_gpsimd, "tensor": p0_tensor,
               "scalar": p0_scalar, "vector": p0_vector})
    if not dry:
        st0.close()

    st1 = ExitStack()
    if not dry:
        st1.__enter__()
    alloc(st1, "sCTX", [128, 8 * D], F32)
    alloc(st1, "sKB", [128, 2 * DC * 512], BF16)
    alloc(st1, "sVB", [128, 2 * 4 * D], BF16)
    alloc(st1, "sPRB", [128, 2 * 512], BF16)
    alloc(st1, "sPT", [128, 2 * 512], BF16)

    def p1_sync(q):
        q = Chain(q, "hds")
        wtv(q, "cc2", 1)
        wtv(q, "cc3", 1)
        wt(q, "pe", "p0pe")
        if "dbg_q" in dr:
            wt(q, "dv", "p0dv")
            ins = q.dma_start(out=dr["dbg_q"], in_=B.sQT)
            inc(ins, "hds", by=16)
        dbg_kb = "dbg_kb" in dr
        for jc in range(NJC):
            if jc >= 2:
                wt(q, "pe", ("fpv", (jc - 1) * 8 - 1))
            rr, jloc = jc // 2, (jc % 2) * 512
            ins = q.dma_start(
                out=B.sKB[:, (jc % 2) * DC * 512:(jc % 2 + 1) * DC * 512].rearrange(
                    "p (c t) -> p c t", t=512),
                in_=dr["k_all"][rr * D:(rr + 1) * D, jloc:jloc + 512].rearrange(
                    "(c p) t -> p c t", p=128),
            )
            inc(ins, "hds", ("kb", jc), by=16)
            ins = q.dma_start(
                out=B.sVB[:, (jc % 2) * 4 * D:(jc % 2 + 1) * 4 * D].rearrange(
                    "p (a d) -> p a d", d=D),
                in_=dr["v_all"][jc * 512:(jc + 1) * 512, :].rearrange(
                    "(a p) d -> p a d", p=128),
            )
            inc(ins, "hds", ("vb", jc), by=16)
            if dbg_kb and jc == 0:
                ins = q.dma_start(out=dr["dbg_kb"], in_=B.sKB[:, 0:DC * 512])
                inc(ins, "hds", by=16)
                ins = q.dma_start(out=dr["dbg_vb"], in_=B.sVB[:, 0:4 * D])
                inc(ins, "hds", by=16)
                wt(q, "dv", ("fstat", 0))
                ins = q.dma_start(out=dr["dbg_st0"], in_=B.sMN[:, 0:1])
                inc(ins, "hds", by=16)
                ins = q.dma_start(out=dr["dbg_st1"], in_=B.sMD[:, 0:1])
                inc(ins, "hds", by=16)
                wt(q, "ac", ("pexp", 0))
                ins = q.dma_start(out=dr["dbg_pr"], in_=B.sPRB[:, 0:512])
                inc(ins, "hds", by=16)
                ins = q.dma_start(out=dr["dbg_st2"], in_=B.sRS[:, 0:1])
                inc(ins, "hds", by=16)
                ins = q.dma_start(out=dr["dbg_st3"], in_=B.sAL[:, 0:1])
                inc(ins, "hds", by=16)
            if dbg_kb and jc == 1:
                wt(q, "ac", ("pexp", 8))
                ins = q.dma_start(out=dr["dbg_st4"], in_=B.sMN[:, 0:1])
                inc(ins, "hds", by=16)
                ins = q.dma_start(out=dr["dbg_st5"], in_=B.sAL[:, 0:1])
                inc(ins, "hds", by=16)
                ins = q.dma_start(out=dr["dbg_st6"], in_=B.sRS[:, 0:1])
                inc(ins, "hds", by=16)
                ins = q.dma_start(out=dr["dbg_st7"], in_=B.sPRB[:, 0:512])
                inc(ins, "hds", by=16)

    def p1_tensor(t):
        wt(t, "ac", "p0ac")
        wt(t, "dv", "p0dv")
        for m in range(NM + 1):
            jc, it = divmod(m, 8)
            if m < NM:
                if m < 8 or it == 0:
                    wt(t, "hds", ("kb", jc))
                if m >= 2:
                    wt(t, "ac", ("pexp", m - 2))
                ins = None
                for dc in range(DC):
                    ins = t.matmul(pp[m % 2][:, :],
                                   B.sQT[:, dc * T + it * 128: dc * T + it * 128 + 128],
                                   B.sKB[:, (jc % 2) * DC * 512 + dc * 512:(jc % 2) * DC * 512 + (dc + 1) * 512],
                                   start=(dc == 0), stop=(dc == DC - 1))
                inc(ins, "pe", ("fsc", m))
            if m >= 1:
                mp = m - 1
                jcp, itp = divmod(mp, 8)
                wt(t, "ac", ("pexp", mp))
                if mp >= 1:
                    wt(t, "dv", ("ptev", mp - 1))
                ins = None
                for qtr in range(4):
                    ins = t.transpose(pb[:, qtr * 128:(qtr + 1) * 128],
                                      B.sPRB[:, (mp % 2) * 512 + qtr * 128:(mp % 2) * 512 + (qtr + 1) * 128],
                                      B.sIDB)
                inc(ins, "pe", ("ftr", mp))
                wt(t, "dv", ("ptev", mp))
                if mp >= 1:
                    wt(t, "dv", ("fpvev", mp - 1))
                if mp < 8 or itp == 0:
                    wt(t, "hds", ("vb", jcp))
                for osl in range(4):
                    ins = None
                    for jsub in range(4):
                        ins = t.matmul(pp[2 + osl][:, :],
                                       B.sPT[:, (mp % 2) * 512 + jsub * 128:(mp % 2) * 512 + (jsub + 1) * 128],
                                       B.sVB[:, (jcp % 2) * 4 * D + jsub * D + osl * 512:(jcp % 2) * 4 * D + jsub * D + (osl + 1) * 512],
                                       start=(jsub == 0), stop=(jsub == 3))
                    if osl == 3:
                        inc(ins, "pe", ("fpv", mp))

    def p1_scalar(s):
        wt(s, "dv", "p0dv")
        for m in range(NM):
            jc, it = divmod(m, 8)
            wt(s, "dv", ("fstat", m))
            if m >= 2:
                wt(s, "pe", ("ftr", m - 2))
            ins = s.activation(out=B.sAL[:, it:it + 1], in_=B.sMD[:, it:it + 1], func=ACTF.Exp)
            ins = s.activation(out=B.sPRB[:, (m % 2) * 512:(m % 2) * 512 + 512],
                               in_=pp[m % 2][:, :], func=ACTF.Exp,
                               bias=B.sMN[:, it:it + 1],
                               accum_out=B.sRS[:, it:it + 1])
            inc(ins, "ac", ("pexp", m))

    def p1_vector(v):
        wt(v, "pe", "p0pe")
        wt(v, "hds", ("vin", 7))
        if not dry:
            v.memset(B.sMN, 3.0e38)
            v.memset(B.sLL, 0.0)
            v.memset(B.sCTX, 0.0)
        ins = v.memset(B.sRS, 0.0) if not dry else None
        inc(ins, "dv", "p1init")
        for m in range(NM):
            jc, it = divmod(m, 8)
            wt(v, "pe", ("fsc", m))
            ins = v.reduce_max(B.sMJ[:, it:it + 1], pp[m % 2][:, :], axis=AXX, negate=True) if not dry else None
            drain(v, ins)
            ins = v.tensor_tensor(B.sMD[:, it:it + 1], B.sMJ[:, it:it + 1], B.sMN[:, it:it + 1], ALU.min) if not dry else None
            drain(v, ins)
            ins = v.tensor_sub(B.sMJ[:, it:it + 1], B.sMD[:, it:it + 1], B.sMN[:, it:it + 1]) if not dry else None
            drain(v, ins)
            if not dry:
                v.tensor_copy(B.sMN[:, it:it + 1], B.sMD[:, it:it + 1])
            ins = v.tensor_copy(B.sMD[:, it:it + 1], B.sMJ[:, it:it + 1]) if not dry else None
            inc(ins, "dv", ("fstat", m))
            wt(v, "ac", ("pexp", m))
            ins = v.scalar_tensor_tensor(B.sLL[:, it:it + 1], B.sLL[:, it:it + 1],
                                         B.sAL[:, it:it + 1], B.sRS[:, it:it + 1],
                                         ALU.mult, ALU.add) if not dry else None
            inc(ins, "dv")
            wt(v, "pe", ("ftr", m))
            if m >= 2:
                wt(v, "pe", ("fpv", m - 1))
            ins = v.tensor_copy(B.sPT[:, (m % 2) * 512:(m % 2) * 512 + 512], pb[:, 0:512]) if not dry else None
            inc(ins, "dv", ("ptev", m))
            wt(v, "pe", ("fpv", m))
            ins = None
            for osl in range(4):
                ins = v.scalar_tensor_tensor(
                    B.sCTX[:, it * D + osl * 512: it * D + (osl + 1) * 512],
                    B.sCTX[:, it * D + osl * 512: it * D + (osl + 1) * 512],
                    B.sAL[:, it:it + 1], pp[2 + osl][:, :],
                    ALU.mult, ALU.add) if not dry else None
            inc(ins, "dv", ("fpvev", m))
        ins = v.reciprocal(B.sRL, B.sLL) if not dry else None
        drain(v, ins)
        ins = None
        for it in range(8):
            ins = v.tensor_scalar(B.sQT[:, it * D:(it + 1) * D],
                                  B.sCTX[:, it * D:(it + 1) * D],
                                  B.sRL[:, it:it + 1], None, ALU.mult) if not dry else None
        inc(ins, "dv", "ctxb")

    run_phase({"sync": p1_sync, "tensor": p1_tensor, "scalar": p1_scalar,
               "vector": p1_vector})
    if not dry:
        st1.close()

    st2a = ExitStack()
    if not dry:
        st2a.__enter__()
    alloc(st2a, "sX2", [128, DC * T], F32)
    alloc(st2a, "sW1B", [128, 2 * DC * 128], BF16)
    alloc(st2a, "sW2B", [128, 2 * FC * 128], BF16)
    if not dry:
        B.sWO = B.sW1B
    st2b = ExitStack()
    if not dry:
        st2b.__enter__()
    alloc(st2b, "sCTXT", [128, DC * T], BF16)
    alloc_pack(st2b, "sXT2", 1024, [("sXSQ2", 0, 1024), ("sTMP", 0, 512)], F32)
    alloc(st2b, "sSTAT2", [1, 6144], F32, attr="sSTAT")

    def p2_sync(q):
        q = Chain(q, "hds")
        wt(q, "dv", "ctxb")
        wt(q, "pe", ("fpv", NM - 1))
        ins = q.dma_start(
            out=B.sX2.rearrange("p (c t) -> p c t", t=T),
            in_=dr["xT"].rearrange("(c p) t -> p c t", p=128),
        )
        inc(ins, "hds", "xload2", by=16)
        if "dbg_x2" in dr:
            wt(q, "dv", ("opjev", 15, 1))
            ins = q.dma_start(out=dr["dbg_x2"], in_=B.sX2)
            inc(ins, "hds", "dbgx2", by=16)

    def p2_gpsimd(g):
        g = Chain(g, "ldg")
        wt(g, "dv", "ctxb")
        wt(g, "pe", ("fpv", NM - 1))
        for oc2 in range(16):
            if oc2 >= 2:
                wt(g, "pe", ("oproj", oc2 - 2))
            ins = g.dma_start(
                out=B.sWO[:, (oc2 % 2) * DC * 128:(oc2 % 2 + 1) * DC * 128].rearrange(
                    "p (c f) -> p c f", f=128),
                in_=dr["woutT"][:, oc2 * 128:(oc2 + 1) * 128].rearrange(
                    "(c p) f -> p c f", p=128),
            )
            inc(ins, "ldg", ("wo", oc2), by=16)
        for e in range(NE + 1):
            if e < NE:
                for fc in range(FC):
                    w = e * FC + fc
                    if w >= 2:
                        ep, fcp = divmod(w - 2, FC)
                        wt(g, "pe", ("hid", ep, fcp, 1))
                    else:
                        wt(g, "pe", ("oproj", 14 + w))
                    src = dr["sw1"][e] if e < 2 else dr["rw1"][e - 2]
                    lane = FFN_LANE[("w1", e, fc)]
                    ins = g.dma_lane(
                        lane,
                        out=B.sW1B[:, (w % 2) * DC * 128:(w % 2 + 1) * DC * 128].rearrange(
                            "p (c f) -> p c f", f=128),
                        in_=src[:, fc * 128:(fc + 1) * 128].rearrange("(c p) f -> p c f", p=128),
                    )
                    inc(ins, lane, ("w1", e, fc), by=16)
            if e >= 1:
                e2 = e - 1
                for dc in range(DC):
                    w = e2 * DC + dc
                    if w >= 2:
                        ep, dcp = divmod(w - 2, DC)
                        wt(g, "pe", ("w2", ep, dcp, 1))
                    src = dr["sw2"][e2] if e2 < 2 else dr["rw2"][e2 - 2]
                    lane = FFN_LANE[("w2l", e2, dc)]
                    ins = g.dma_lane(
                        lane,
                        out=B.sW2B[:, (w % 2) * FC * 128:(w % 2 + 1) * FC * 128].rearrange(
                            "p (c f) -> p c f", f=128),
                        in_=src[:, dc * 128:(dc + 1) * 128].rearrange("(c p) f -> p c f", p=128),
                    )
                    inc(ins, lane, ("w2l", e2, dc), by=16)

    def p2_tensor(t):
        wt(t, "dv", "ctxb")
        for it in range(8):
            for bb in range(4):
                bidx = it * 4 + bb
                if bidx >= 1:
                    wt(t, "dv", ("ctrv", bidx - 1))
                ins = None
                for k in range(4):
                    dc = bb * 4 + k
                    ins = t.transpose(pb[:, k * 128:(k + 1) * 128],
                                      B.sQT[:, it * D + dc * 128: it * D + (dc + 1) * 128],
                                      B.sIDB)
                inc(ins, "pe", ("ctr", bidx))
        wt(t, "dv", ("ctrv", 31))
        for oc2 in range(16):
            wt(t, "ldg", ("wo", oc2))
            wslab = B.sWO[:, (oc2 % 2) * DC * 128:(oc2 % 2 + 1) * DC * 128]
            for tch in range(2):
                bank = (oc2 * 2 + tch) % 4
                if oc2 >= 2:
                    wt(t, "dv", ("opjev", oc2 - 2, tch))
                ins = None
                for dc in range(DC):
                    ins = t.matmul(pp[bank][:, :], wslab[:, dc * 128:(dc + 1) * 128],
                                   B.sCTXT[:, dc * T + tch * 512: dc * T + tch * 512 + 512],
                                   start=(dc == 0), stop=(dc == DC - 1))
                inc(ins, "pe", ("opj", oc2, tch), *((("oproj", oc2),) if tch == 1 else ()))
        wt(t, "dv", ("opjev", 15, 1))
        for tch in range(2):
            for dc in range(DC):
                k = tch * DC + dc
                xs = B.sX2[:, dc * T + tch * 512: dc * T + tch * 512 + 512]
                ins = t.matmul(pp[4 + tch][0:1, :], B.sONESC, xs,
                               start=(dc == 0), stop=(dc == DC - 1))
                wt(t, "ac", ("xsq2", k))
                ins = t.matmul(pp[0 + tch][0:1, :], B.sONESC,
                               B.sXSQ2[:, (k % 2) * 512:(k % 2) * 512 + 512],
                               start=(dc == 0), stop=(dc == DC - 1))
                inc(ins, "pe", ("sqmm2", k))
        wt(t, "dv", "ab2")
        bank_ab2 = {(0, 0): 2, (0, 1): 3, (1, 0): 4, (1, 1): 5}
        ins = None
        for which in range(2):
            for tch in range(2):
                ins = t.matmul(pp[bank_ab2[(which, tch)]][:, :], B.sONESR[0:1, 0:128],
                               B.sSTAT[0:1, 3072 + which * 1024 + tch * 512: 3072 + which * 1024 + (tch + 1) * 512],
                               start=True, stop=True)
        inc(ins, "pe", "bc2", "p2pe")

    def p2_scalar(s):
        wt(s, "dv", "ctxb")
        wt(s, "hds", "xload2")
        for tch in range(2):
            for dc in range(DC):
                k = tch * DC + dc
                if k >= 2:
                    wt(s, "pe", ("sqmm2", k - 2))
                wt(s, "dv", ("opjev", dc, tch))
                ins = s.activation(out=B.sXSQ2[:, (k % 2) * 512:(k % 2) * 512 + 512],
                                   in_=B.sX2[:, dc * T + tch * 512: dc * T + tch * 512 + 512],
                                   func=ACTF.Square)
                inc(ins, "ac", ("xsq2", k))
        wt(s, "dv", "var2")
        ins = s.activation(out=B.sSTAT[0:1, 2048:3072], in_=B.sSTAT[0:1, 4096:5120],
                           func=ACTF.Sqrt)
        inc(ins, "ac", "sqrt2")

    def p2_vector(v):
        for bidx in range(32):
            it, bb = divmod(bidx, 4)
            wt(v, "pe", ("ctr", bidx))
            ins = v.tensor_copy(
                B.sCTXT.rearrange("p (c t) -> p c t", t=T)[:, bb * 4:(bb + 1) * 4, it * 128:(it + 1) * 128],
                pb[:, 0:512].rearrange("p (k q) -> p k q", q=128),
            ) if not dry else None
            inc(ins, "dv", ("ctrv", bidx))
        wt(v, "hds", "xload2")
        for dc in range(DC):
            ins = v.tensor_scalar(B.sX2[:, dc * T:(dc + 1) * T],
                                  B.sX2[:, dc * T:(dc + 1) * T],
                                  B.sGAB[:, dc:dc + 1], None, ALU.add) if not dry else None
            inc(ins, "dv", ("xbias", dc))
        for oc2 in range(16):
            for tch in range(2):
                wt(v, "pe", ("opj", oc2, tch))
                ins = v.scalar_tensor_tensor(
                    B.sX2[:, oc2 * T + tch * 512: oc2 * T + tch * 512 + 512],
                    pp[(oc2 * 2 + tch) % 4][:, :], B.sGA[:, oc2:oc2 + 1],
                    B.sX2[:, oc2 * T + tch * 512: oc2 * T + tch * 512 + 512],
                    ALU.mult, ALU.add) if not dry else None
                inc(ins, "dv", ("opjev", oc2, tch))
        wt(v, "pe", ("sqmm2", 31))
        if not dry:
            v.tensor_scalar(B.sSTAT[0:1, 0:512], pp[4][0:1, :], 1.0 / D, None, ALU.mult)
            v.tensor_scalar(B.sSTAT[0:1, 512:1024], pp[5][0:1, :], 1.0 / D, None, ALU.mult)
            v.tensor_scalar(B.sSTAT[0:1, 3072:3584], pp[0][0:1, :], 1.0 / D, None, ALU.mult)
            v.tensor_scalar(B.sSTAT[0:1, 3584:4096], pp[1][0:1, :], 1.0 / D, None, ALU.mult)
            v.tensor_mul(B.sSTAT[0:1, 4096:5120], B.sSTAT[0:1, 0:1024], B.sSTAT[0:1, 0:1024])
            v.tensor_sub(B.sSTAT[0:1, 1024:2048], B.sSTAT[0:1, 3072:4096], B.sSTAT[0:1, 4096:5120])
            v.tensor_scalar(B.sSTAT[0:1, 1024:2048], B.sSTAT[0:1, 1024:2048], EPS, None, ALU.add)
        ins = v.reciprocal(B.sSTAT[0:1, 4096:5120], B.sSTAT[0:1, 1024:2048]) if not dry else None
        inc(ins, "dv", "var2")
        wt(v, "ac", "sqrt2")
        if not dry:
            r0 = B.sSTAT[0:1, 2048:3072]
            v.tensor_mul(B.sSTAT[0:1, 5120:6144], r0, r0)
            v.tensor_mul(B.sSTAT[0:1, 5120:6144], B.sSTAT[0:1, 5120:6144], B.sSTAT[0:1, 1024:2048])
            v.tensor_scalar(B.sSTAT[0:1, 5120:6144], B.sSTAT[0:1, 5120:6144], -0.5, 1.5, ALU.mult, ALU.add)
            v.tensor_mul(B.sSTAT[0:1, 3072:4096], B.sSTAT[0:1, 5120:6144], r0)
            v.scalar_tensor_tensor(B.sSTAT[0:1, 4096:5120], B.sSTAT[0:1, 0:1024], -1.0,
                                   B.sSTAT[0:1, 3072:4096], ALU.mult, ALU.mult)
        ins = v.tensor_copy(B.sSTAT[0:1, 5120:5121], B.sSTAT[0:1, 3072:3073]) if not dry else None
        inc(ins, "dv", "ab2")
        wt(v, "pe", "bc2")
        if "dbg_x2" in dr:
            wt(v, "hds", "dbgx2")
        bank_ab2 = {(0, 0): 2, (0, 1): 3, (1, 0): 4, (1, 1): 5}
        ins = None
        for dc in range(DC):
            for tch in range(2):
                sl = slice(dc * T + tch * 512, dc * T + tch * 512 + 512)
                if not dry:
                    v.tensor_mul(B.sTMP, B.sX2[:, sl], pp[bank_ab2[(0, tch)]][:, :])
                    v.tensor_add(B.sTMP, B.sTMP, pp[bank_ab2[(1, tch)]][:, :])
                    v.tensor_scalar(B.sH[:, sl], B.sTMP, B.sCB1[:, dc:dc + 1],
                                    B.sMODT[:, 48 + dc:49 + dc], ALU.mult, ALU.add)
                    v.scalar_tensor_tensor(B.sX2[:, sl], B.sH[:, sl], B.sGB[:, dc:dc + 1],
                                           B.sX2[:, sl], ALU.mult, ALU.add)
                ins = v.tensor_scalar(B.sX2[:, sl], B.sX2[:, sl], B.sGBS2[:, dc:dc + 1],
                                      None, ALU.add) if not dry else None
        inc(ins, "dv", "hdone2", "p2dv")

    run_phase({"sync": p2_sync, "gpsimd": p2_gpsimd, "tensor": p2_tensor,
               "scalar": p2_scalar, "vector": p2_vector})
    if not dry:
        st2b.close()

    st3 = ExitStack()
    if not dry:
        st3.__enter__()
    alloc(st3, "sHID", [128, 2 * FC * T], BF16)
    alloc(st3, "sGBC", [128, 2 * T], BF16)
    alloc(st3, "sESEL", [16, 16 * 128], F32)
    alloc(st3, "sTKS", [128, 64], F32)
    alloc(st3, "sSCT", [128, 8 * 16], F32)
    alloc(st3, "sGT", [128, 8 * 16], F32)
    alloc(st3, "sSCO", [16, T], F32)
    if not dry:
        B.sGTT = B.sSCO

    def p3_sync(q):
        q = Chain(q, "hds")
        wt(q, "dv", "p2dv")
        if "dbg_h2" in dr:
            ins = q.dma_start(out=dr["dbg_fst"], in_=B.sMN)
            inc(ins, "hds", by=16)
            ins = q.dma_start(out=dr["dbg_fst2"], in_=B.sLL)
            inc(ins, "hds", by=16)
            ins = q.dma_start(out=dr["dbg_ctx"], in_=B.sQT)
            inc(ins, "hds", by=16)
            ins = q.dma_start(out=dr["dbg_mod"], in_=dr["mod_all"])
            inc(ins, "hds", by=16)
            ins = q.dma_start(out=dr["dbg_kall"], in_=dr["k_all"])
            inc(ins, "hds", by=16)
            ins = q.dma_start(out=dr["dbg_vall"], in_=dr["v_all"])
            inc(ins, "hds", by=16)
            ins = q.dma_start(out=dr["dbg_h2"], in_=B.sH)
            inc(ins, "hds", by=16)
            wt(q, "dv", ("gttv", 7))
            ins = q.dma_start(out=dr["dbg_gtt"], in_=B.sGTT)
            inc(ins, "hds", by=16)
        ins = q.dma_start(out=B.sESEL, in_=dr["esel"])
        inc(ins, "hds", "p3sm", by=16)
        wt(q, "pe", ("gbc", NE - 1))
        ins = q.dma_start(out=B.sESEL[0:16, 0:D], in_=dr["rb2"])
        inc(ins, "hds", "p3rb2", by=16)
        wt(q, "dv", ("b2ev", 15, 1))
        ins = q.dma_start(
            out=dr["y"].rearrange("(c p) t -> p c t", p=128),
            in_=B.sX2.rearrange("p (c t) -> p c t", t=T),
        )
        inc(ins, "hds", "ystore", by=16)
        wtv(q, "hds", S.n.get("hds", 0) if dry else S.at("hds", "ystore"))

    def p3_tensor(t):
        wt(t, "dv", "p2dv")
        for tch in range(2):
            ins = None
            for dc in range(DC):
                ins = t.matmul(pp[4 + tch][0:16, :], B.sCENTB[:, dc * 16:(dc + 1) * 16],
                               B.sH[:, dc * T + tch * 512: dc * T + tch * 512 + 512],
                               start=(dc == 0), stop=(dc == DC - 1))
            inc(ins, "pe", ("scot", tch))
        wt(t, "dv", "scob")
        for tt in range(8):
            if tt >= 1:
                wt(t, "dv", ("sctv", tt - 1))
            ins = t.transpose(pp[6][:, 0:16], B.sSCO[0:16, tt * 128:(tt + 1) * 128],
                              B.sIDF[0:16, 0:16])
            inc(ins, "pe", ("sct", tt))
        for tt in range(8):
            wt(t, "dv", ("top4", tt))
            if tt >= 1:
                wt(t, "dv", ("gttv", tt - 1))
            else:
                wt(t, "dv", ("sctv", 7))
            ins = t.transpose(pp[6][0:16, 0:128], B.sGT[:, tt * 16:(tt + 1) * 16],
                              B.sIDF)
            inc(ins, "pe", ("gtt", tt))
        for e in range(NE + 1):
            if e < NE:
                if e >= 2:
                    j = e - 2
                    wt(t, "dv", ("gttv", 7))
                    if e == 2:
                        wt(t, "ac", ("scos", 1))
                        wt(t, "hds", "p3sm")
                    else:
                        wt(t, "dv", ("gbcv", e - 1))
                    ins = None
                    for tch in range(2):
                        ins = t.matmul(pp[4 + tch][:, :], B.sESEL[0:16, j * 128:(j + 1) * 128],
                                       B.sGTT[0:16, tch * 512:(tch + 1) * 512],
                                       start=True, stop=True)
                    inc(ins, "pe", ("gbc", e))
                for fc in range(FC):
                    wt(t, FFN_LANE[("w1", e, fc)], ("w1", e, fc))
                    wslab = B.sW1B[:, ((e * FC + fc) % 2) * DC * 128:((e * FC + fc) % 2 + 1) * DC * 128]
                    for tch in range(2):
                        if e >= 1 or fc >= 1:
                            pfc = fc - 1 if fc >= 1 else FC - 1
                            pe_ = e if fc >= 1 else e - 1
                            wt(t, "ac", ("hrelu", pe_, pfc, tch))
                        ins = None
                        for dc in range(DC):
                            ins = t.matmul(pp[tch][:, :], wslab[:, dc * 128:(dc + 1) * 128],
                                           B.sH[:, dc * T + tch * 512: dc * T + tch * 512 + 512],
                                           start=(dc == 0), stop=(dc == DC - 1))
                        inc(ins, "pe", ("hid", e, fc, tch))
            if e >= 1:
                ep = e - 1
                if ep >= 2:
                    wt(t, "dv", ("hscale", ep, FC - 1))
                else:
                    wt(t, "ac", ("hrelu", ep, FC - 1, 1))
                for dc in range(DC):
                    wt(t, FFN_LANE[("w2l", ep, dc)], ("w2l", ep, dc))
                    wslab = B.sW2B[:, ((ep * DC + dc) % 2) * FC * 128:((ep * DC + dc) % 2 + 1) * FC * 128]
                    for tch in range(2):
                        if ep >= 1 or dc >= 1:
                            pdc = dc - 1 if dc >= 1 else DC - 1
                            pep = ep if dc >= 1 else ep - 1
                            wt(t, "dv", ("w2ev", pep, pdc, tch))
                        ins = None
                        for fc in range(FC):
                            ins = t.matmul(pp[2 + tch][:, :], wslab[:, fc * 128:(fc + 1) * 128],
                                           B.sHID[:, (ep % 2) * FC * T + fc * T + tch * 512:(ep % 2) * FC * T + fc * T + tch * 512 + 512],
                                           start=(fc == 0), stop=(fc == FC - 1))
                        inc(ins, "pe", ("w2", ep, dc, tch))
        wt(t, "dv", ("gbcv", NE - 1))
        wt(t, "hds", "p3rb2")
        for dc in range(DC):
            for tch in range(2):
                if dc >= 1:
                    wt(t, "dv", ("b2ev", dc - 1, tch))
                ins = t.matmul(pp[4 + tch][:, :], B.sESEL[0:16, dc * 128:(dc + 1) * 128],
                               B.sGTT[0:16, tch * 512:(tch + 1) * 512],
                               start=True, stop=True)
                inc(ins, "pe", ("b2", dc, tch))

    def p3_scalar(s):
        wt(s, "pe", "p2pe")
        wt(s, "dv", "p2dv")
        for tch in range(2):
            wt(s, "pe", ("scot", tch))
            ins = s.activation(out=B.sSCO[0:16, tch * 512:(tch + 1) * 512],
                               in_=pp[4 + tch][0:16, :], func=ACTF.Sigmoid)
            inc(ins, "ac", ("scos", tch))
        for e in range(NE):
            for fc in range(FC):
                for tch in range(2):
                    wt(s, "pe", ("hid", e, fc, tch))
                    if e >= 2 and fc == 0 and tch == 0:
                        wt(s, "pe", ("w2", e - 2, DC - 1, 1))
                    if e < 2:
                        bias = B.sSB1[:, e * FC + fc: e * FC + fc + 1]
                    else:
                        bias = B.sRB1[:, (e - 2) * FC + fc: (e - 2) * FC + fc + 1]
                    ins = s.activation(
                        out=B.sHID[:, (e % 2) * FC * T + fc * T + tch * 512:(e % 2) * FC * T + fc * T + tch * 512 + 512],
                        in_=pp[tch][:, :], func=ACTF.Relu, bias=bias)
                    inc(ins, "ac", ("hrelu", e, fc, tch))

    def p3_vector(v):
        wt(v, "pe", "p2pe")
        wt(v, "ac", ("scos", 1))
        ins = v.tensor_scalar(B.sSCO, B.sSCO, B.sFB, None, ALU.add) if not dry else None
        inc(ins, "dv", "scob")
        for tt in range(8):
            wt(v, "pe", ("sct", tt))
            ins = v.tensor_copy(B.sSCT[:, tt * 16:(tt + 1) * 16], pp[6][:, 0:16]) if not dry else None
            inc(ins, "dv", ("sctv", tt))
        for tt in range(8):
            ssl = B.sSCT[:, tt * 16:(tt + 1) * 16]
            gsl = B.sGT[:, tt * 16:(tt + 1) * 16]
            if not dry:
                w = B.sTKS[:, 0:16]
                mcol = B.sTKS[:, 16:17]
                ge = B.sTKS[:, 32:48]
            ins = v.tensor_copy(w, ssl) if not dry else None
            drain(v, ins)
            for rnd in range(3):
                ins = v.reduce_max(mcol, w, axis=AXX) if not dry else None
                drain(v, ins)
                ins = v.tensor_scalar(ge, w, mcol, None, ALU.is_ge) if not dry else None
                drain(v, ins)
                ins = v.scalar_tensor_tensor(w, ge, BIGNEG, w, ALU.mult, ALU.add) if not dry else None
                drain(v, ins)
            ins = v.reduce_max(mcol, w, axis=AXX) if not dry else None
            drain(v, ins)
            ins = v.tensor_scalar(ge, ssl, mcol, None, ALU.is_ge) if not dry else None
            drain(v, ins)
            ins = v.tensor_mul(gsl, ge, ssl) if not dry else None
            drain(v, ins)
            ins = v.reduce_sum(mcol, gsl, axis=AXX) if not dry else None
            drain(v, ins)
            ins = v.tensor_scalar_max(mcol, mcol, 1e-8) if not dry else None
            drain(v, ins)
            ins = v.reciprocal(B.sTKS[:, 17:18], mcol) if not dry else None
            drain(v, ins)
            ins = v.tensor_scalar(gsl, gsl, B.sTKS[:, 17:18], None, ALU.mult) if not dry else None
            inc(ins, "dv", ("top4", tt))
        for tt in range(8):
            wt(v, "pe", ("gtt", tt))
            ins = v.tensor_copy(B.sGTT[0:16, tt * 128:(tt + 1) * 128], pp[6][0:16, 0:128]) if not dry else None
            inc(ins, "dv", ("gttv", tt))
        for e in range(NE + 1):
            if e < NE and e >= 2:
                wt(v, "pe", ("gbc", e))
                ins = None
                for tch in range(2):
                    ins = v.tensor_copy(B.sGBC[:, (e % 2) * T + tch * 512:(e % 2) * T + tch * 512 + 512],
                                        pp[4 + tch][:, :]) if not dry else None
                inc(ins, "dv", ("gbcv", e))
                for fc in range(FC):
                    wt(v, "ac", ("hrelu", e, fc, 1))
                    hsl = B.sHID[:, (e % 2) * FC * T + fc * T:(e % 2) * FC * T + (fc + 1) * T]
                    ins = v.tensor_mul(hsl, hsl, B.sGBC[:, (e % 2) * T:(e % 2) * T + T]) if not dry else None
                    inc(ins, "dv", ("hscale", e, fc))
            elif e < NE:
                inc(None, "dv", ("gbcv", e), by=0)
            if e >= 1:
                ep = e - 1
                for dc in range(DC):
                    for tch in range(2):
                        wt(v, "pe", ("w2", ep, dc, tch))
                        sl = slice(dc * T + tch * 512, dc * T + tch * 512 + 512)
                        ins = v.scalar_tensor_tensor(
                            B.sX2[:, sl], pp[2 + tch][:, :], B.sGB[:, dc:dc + 1],
                            B.sX2[:, sl], ALU.mult, ALU.add) if not dry else None
                        inc(ins, "dv", ("w2ev", ep, dc, tch))
        for dc in range(DC):
            for tch in range(2):
                wt(v, "pe", ("b2", dc, tch))
                sl = slice(dc * T + tch * 512, dc * T + tch * 512 + 512)
                ins = v.scalar_tensor_tensor(
                    B.sX2[:, sl], pp[4 + tch][:, :], B.sGB[:, dc:dc + 1],
                    B.sX2[:, sl], ALU.mult, ALU.add) if not dry else None
                inc(ins, "dv", ("b2ev", dc, tch))

    run_phase({"sync": p3_sync, "tensor": p3_tensor, "scalar": p3_scalar,
               "vector": p3_vector})
    if not dry:
        st3.close()
        st2a.close()

    if not S.frozen:
        S.c[("pe", "p0pe")] = S.c[("pe", ("qkv", OQ - 1))]
        S.c[("dv", "p0dv")] = S.c[("dv", ("kev", 15, 1))]
        S.c[("ac", "p0ac")] = S.c[("ac", ("vevB", 47))]


def build_nc():
    nc = bass.Bass("TRN2", target_bir_lowering=False, debug=False,
                   detect_race_conditions=False)
    dr = {}

    def di(name, shape, dt=F32):
        dr[name] = nc.dram_tensor(name, shape, dt, kind="ExternalInput").ap()

    di("xT", [D, T])
    di("wqkvT", [D, 3 * D])
    di("woutT", [D, D])
    di("centT", [D, 16])
    di("c2", [128, 16])
    di("bqkv2", [128, OQ])
    di("bqvr", [1, D])
    di("bout2", [128, 16])
    di("sb1_2", [128, 22])
    di("rb1_2", [128, 176])
    di("sb2_2", [128, 32])
    di("rb2", [16, D])
    di("fbias", [16, 1])
    di("ones", [1, 128])
    di("ones_col", [128, 1])
    di("ident", [128, 128])
    di("esel", [16, 16 * 128])
    di("wmod_sl", [D, MODSL])
    di("sw1", [2, D, DFF])
    di("sw2", [2, DFF, D])
    di("rw1", [16, D, DFF])
    di("rw2", [16, DFF, D])
    dr["y"] = nc.dram_tensor("y", [D, T], F32, kind="ExternalOutput").ap()
    import os as _os
    if _os.environ.get("KDBG"):
        dr["dbg_mod"] = nc.dram_tensor("dbg_mod", [R, MODSL], F32, kind="ExternalOutput").ap()
        dr["dbg_kall"] = nc.dram_tensor("dbg_kall", [R * D, T], BF16, kind="ExternalOutput").ap()
        dr["dbg_vall"] = nc.dram_tensor("dbg_vall", [R * T, D], BF16, kind="ExternalOutput").ap()
        dr["dbg_h2"] = nc.dram_tensor("dbg_h2", [128, DC * T], BF16, kind="ExternalOutput").ap()
        dr["dbg_gtt"] = nc.dram_tensor("dbg_gtt", [16, T], F32, kind="ExternalOutput").ap()
        dr["dbg_x2"] = nc.dram_tensor("dbg_x2", [128, DC * T], F32, kind="ExternalOutput").ap()
        dr["dbg_q"] = nc.dram_tensor("dbg_q", [128, DC * T], BF16, kind="ExternalOutput").ap()
        dr["dbg_ctx"] = nc.dram_tensor("dbg_ctx", [128, DC * T], BF16, kind="ExternalOutput").ap()
        dr["dbg_kb"] = nc.dram_tensor("dbg_kb", [128, DC * 512], BF16, kind="ExternalOutput").ap()
        dr["dbg_vb"] = nc.dram_tensor("dbg_vb", [128, 4 * D], BF16, kind="ExternalOutput").ap()
        dr["dbg_sc"] = nc.dram_tensor("dbg_sc", [128, 512], F32, kind="ExternalOutput").ap()
        dr["dbg_pr"] = nc.dram_tensor("dbg_pr", [128, 512], BF16, kind="ExternalOutput").ap()
        for i in range(7):
            dr[f"dbg_st{i}"] = nc.dram_tensor(f"dbg_st{i}", [128, 1], F32, kind="ExternalOutput").ap()
        dr["dbg_st7"] = nc.dram_tensor("dbg_st7", [128, 512], BF16, kind="ExternalOutput").ap()
        dr["dbg_fst"] = nc.dram_tensor("dbg_fst", [128, 8], F32, kind="ExternalOutput").ap()
        dr["dbg_fst2"] = nc.dram_tensor("dbg_fst2", [128, 8], F32, kind="ExternalOutput").ap()
    dr["mod_in"] = nc.dram_tensor("mod_in", [1, MODSL], F32, kind="Internal").ap()
    dr["mod_all"] = nc.dram_tensor("mod_all", [R, MODSL], F32, kind="Internal").ap()
    dr["k_in"] = nc.dram_tensor("k_in", [D, T], BF16, kind="Internal").ap()
    dr["k_all"] = nc.dram_tensor("k_all", [R * D, T], BF16, kind="Internal").ap()
    dr["v_in"] = nc.dram_tensor("v_in", [T, D], BF16, kind="Internal").ap()
    dr["v_all"] = nc.dram_tensor("v_all", [R * T, D], BF16, kind="Internal").ap()

    with ExitStack() as ctx:
        sems = {n: ctx.enter_context(nc.semaphore(n))
                for n in ("pe", "dv", "ac", "hds", "ldg", "lgb", "cc1", "cc2", "cc3")}
        S = Sched()
        emit(nc, ctx, dr, sems, S, dry=True)
        S.frozen = True
        with nc.Block() as blk:
            emit(nc, ctx, dr, sems, S, dry=False, blk=blk)
    return nc


def kernel(L0_attn_hidden_states, L0_attn_w_qkv, L0_attn_b_qkv, L0_attn_w_out,
           L0_attn_b_out, L0_ffn_shared_w1, L0_ffn_shared_b1, L0_ffn_shared_w2,
           L0_ffn_shared_b2, L0_ffn_routed_w1, L0_ffn_routed_b1, L0_ffn_routed_w2,
           L0_ffn_routed_b2, L0_ffn_centroids, L0_ffn_bias, L0_c, L0_wmod):
    f = np.float32
    x = np.asarray(L0_attn_hidden_states, f)
    common = {
        "wqkvT": np.ascontiguousarray(np.asarray(L0_attn_w_qkv, f).T),
        "woutT": np.ascontiguousarray(np.asarray(L0_attn_w_out, f).T),
        "centT": np.ascontiguousarray(np.asarray(L0_ffn_centroids, f).T),
        "c2": np.ascontiguousarray(np.asarray(L0_c, f).reshape(16, 128).T),
        "bqkv2": np.ascontiguousarray(np.asarray(L0_attn_b_qkv, f).reshape(OQ, 128).T),
        "bqvr": np.ascontiguousarray(np.asarray(L0_attn_b_qkv, f)[4096:6144].reshape(1, D)),
        "bout2": np.ascontiguousarray(np.asarray(L0_attn_b_out, f).reshape(16, 128).T),
        "sb1_2": np.ascontiguousarray(
            np.asarray(L0_ffn_shared_b1, f).reshape(2, FC, 128).transpose(2, 0, 1).reshape(128, 22)),
        "rb1_2": np.ascontiguousarray(
            np.asarray(L0_ffn_routed_b1, f).reshape(16, FC, 128).transpose(2, 0, 1).reshape(128, 176)),
        "sb2_2": np.ascontiguousarray(
            np.asarray(L0_ffn_shared_b2, f).reshape(2, 16, 128).transpose(2, 0, 1).reshape(128, 32)),
        "rb2": np.ascontiguousarray(np.asarray(L0_ffn_routed_b2, f)),
        "fbias": np.ascontiguousarray(np.asarray(L0_ffn_bias, f).reshape(16, 1)),
        "ones": np.ones((1, 128), f),
        "ones_col": np.ones((128, 1), f),
        "ident": np.eye(128, dtype=f),
        "esel": np.ascontiguousarray(
            np.eye(16, dtype=f)[:, :, None].repeat(128, axis=2).reshape(16, 16 * 128)),
        "sw1": np.ascontiguousarray(np.asarray(L0_ffn_shared_w1, f)),
        "sw2": np.ascontiguousarray(np.asarray(L0_ffn_shared_w2, f)),
        "rw1": np.ascontiguousarray(np.asarray(L0_ffn_routed_w1, f)),
        "rw2": np.ascontiguousarray(np.asarray(L0_ffn_routed_w2, f)),
    }
    wmod = np.asarray(L0_wmod, f)
    in_maps = []
    for r in range(R):
        m = dict(common)
        m["xT"] = np.ascontiguousarray(x[0, r * T:(r + 1) * T, :].T)
        m["wmod_sl"] = np.ascontiguousarray(wmod[:, r * MODSL:(r + 1) * MODSL])
        in_maps.append(m)

    nc = build_nc()
    res = run_bass_kernel_spmd(nc, in_maps, core_ids=list(range(R)))
    out = np.concatenate([np.asarray(res.results[r]["y"]).T for r in range(R)], axis=0)
    return np.ascontiguousarray(out.reshape(1, SG, D), f)


# version 07 (final)
# speedup vs baseline: 1.1791x; 1.0281x over previous
import numpy as np
from contextlib import ExitStack
from types import SimpleNamespace

import concourse.bass as bass
import concourse.mybir as mybir
from concourse.bass_utils import run_bass_kernel_spmd

F32 = mybir.dt.float32
BF16 = mybir.dt.bfloat16
AXX = mybir.AxisListType.X
ALU = mybir.AluOpType
ACTF = mybir.ActivationFunctionType

R = 8
T = 1024
D = 2048
DC = 16
OQ = 48
DFF = 1408
FC = 11
NE = 18
SG = 8192
NJC = 16
NIT = 8
NM = NJC * NIT
MODW = 12288
MODSL = MODW // R
SCALE = float(1.0 / np.sqrt(2048.0))
EPS = 1e-5
BIGNEG = -1.0e30


class Sched:
    def __init__(self):
        self.c = {}
        self.n = {}
        self.frozen = False

    def reset_counts(self):
        self.n = {}

    def inc(self, sem, labels=(), by=1):
        v = self.n.get(sem, 0) + by
        self.n[sem] = v
        for L in labels:
            key = (sem, L)
            if self.frozen:
                assert self.c.get(key) == v, f"sched mismatch {key}: {self.c.get(key)} vs {v}"
            else:
                assert key not in self.c, f"dup label {key}"
                self.c[key] = v
        return v

    def at(self, sem, label):
        return self.c[(sem, label)]


class _DryInst:
    def then_inc(self, *a, **k):
        return self


_DRY_INST = _DryInst()


class _DryEng:
    def __getattr__(self, n):
        if n == "wait_ge":
            return lambda *a, **k: None
        return lambda *a, **k: _DRY_INST


class _DryAP:
    def __getitem__(self, *_):
        return self

    def rearrange(self, *a, **k):
        return self

    def opt(self):
        return self


_DRY_AP = _DryAP()


class _DryB:
    def __getattr__(self, n):
        return _DRY_AP


def emit(nc, ctx, dr, sems, S, dry, blk=None):
    S.reset_counts()

    if dry:
        B = _DryB()
    else:
        B = SimpleNamespace()

    def alloc(st, name, shape, dt, attr=None):
        if not dry:
            setattr(B, attr or name, st.enter_context(nc.sbuf_tensor(name, shape, dt))[:])

    def palloc(st, name, shape, dt):
        if not dry:
            setattr(B, name, st.enter_context(nc.psum_tensor(name, shape, dt))[:])

    def inc(ins, sem, *labels, by=1):
        S.inc(sem, labels, by)
        if not dry and ins is not None:
            ins.then_inc(sems[sem], by)

    def wt(eng, sem, label):
        if dry:
            return
        eng.wait_ge(sems[sem], S.at(sem, label))

    def wtv(eng, sem, v):
        if dry:
            return
        eng.wait_ge(sems[sem], v)

    def drain(eng, ins):
        inc(ins, "dv")
        wtv(eng, "dv", S.n["dv"])

    class Chain:

        def __init__(self, q, semname):
            self._q = q
            self._sem = semname

        def dma_start(self, **kw):
            prev = S.n.get(self._sem, 0)
            if prev and not dry:
                self._q.wait_ge(sems[self._sem], prev)
            return self._q.dma_start(**kw)

        def dma_lane(self, semname, **kw):
            prev = S.n.get(semname, 0)
            if prev and not dry:
                self._q.wait_ge(sems[semname], prev)
            return self._q.dma_start(**kw)

        def __getattr__(self, n):
            return getattr(self._q, n)

    FFN_LANE = {}
    _si = 0
    for _e in range(NE + 1):
        if _e < NE:
            for _fc in range(FC):
                FFN_LANE[("w1", _e, _fc)] = ("ldg", "lgb")[_si % 2]
                _si += 1
        if _e >= 1:
            for _dc in range(DC):
                FFN_LANE[("w2l", _e - 1, _dc)] = ("ldg", "lgb")[_si % 2]
                _si += 1

    def run_phase(fns):
        order = ("sync", "gpsimd", "tensor", "scalar", "vector")
        for name in order:
            if name in fns:
                if dry:
                    fns[name](_DryEng())
                else:
                    getattr(blk, name)(fns[name])

    if not dry:
        per = ctx
    alloc(ctx, "sH", [128, DC * T], BF16)
    alloc(ctx, "sQT", [128, DC * T], BF16)
    alloc(ctx, "sMODT", [128, 96], F32)

    def alloc_pack(st, name, cols, views, dt):
        if not dry:
            h = st.enter_context(nc.sbuf_tensor(name, [128, cols], dt))[:]
            for vn, c0, c1 in views:
                setattr(B, vn, h[:, c0:c1])

    alloc_pack(ctx, "sSM16", 144, [
        ("sCA1", 0, 16), ("sCB1", 16, 32), ("sGA", 32, 48), ("sGB", 48, 64),
        ("sGAB", 64, 80), ("sGBS2", 80, 96), ("sSB2S", 96, 112),
        ("sT16a", 112, 128), ("sT16b", 128, 144)], F32)
    alloc_pack(ctx, "sPKB", 310, [
        ("sBQKV", 0, 48), ("sBOUT", 48, 64), ("sSB1", 64, 86),
        ("sRB1", 86, 262), ("sSB22", 262, 294), ("sC2F", 294, 310)], F32)
    alloc_pack(ctx, "sFST", 56, [
        ("sMN", 0, 8), ("sMJ", 8, 16), ("sMD", 16, 24), ("sAL", 24, 32),
        ("sLL", 32, 40), ("sRS", 40, 48), ("sRL", 48, 56)], F32)
    alloc(ctx, "sONESR", [1, 128], F32)
    alloc(ctx, "sONESC", [128, 1], F32)
    alloc(ctx, "sONESCB", [128, 1], BF16)
    alloc(ctx, "sIDF", [128, 128], F32)
    alloc(ctx, "sIDB", [128, 128], BF16)
    alloc(ctx, "sFB", [16, 1], F32)
    alloc(ctx, "sCENTB", [128, DC * 16], BF16)
    alloc(ctx, "sSCB", [128, 16], F32)
    pp = []
    for i in range(7):
        if not dry:
            pp.append(ctx.enter_context(nc.psum_tensor(f"pp{i}", [128, 512], F32))[:])
        else:
            pp.append(_DRY_AP)
    if not dry:
        pb = ctx.enter_context(nc.psum_tensor("pb", [128, 1024], BF16))[:]
    else:
        pb = _DRY_AP

    y_ap = dr["y"]

    st0 = ExitStack()
    if not dry:
        st0.__enter__()
    alloc(st0, "sX", [128, DC * T], BF16)
    alloc(st0, "sWM", [128, 2 * MODSL], F32)
    alloc(st0, "sXSQ", [128, 2 * 512], F32)
    alloc(st0, "sKST", [128, 2 * T], BF16)
    alloc(st0, "sVAS", [128, 8 * D], BF16)
    alloc(st0, "sWQ", [128, 2 * DC * 128], BF16)
    alloc(st0, "sMROW", [1, MODSL], F32)
    alloc(st0, "sMLD", [96, 128], F32)
    alloc(st0, "sBQR", [1, 2048], F32)
    alloc(st0, "sSTAT", [1, 6144], F32)
    alloc(st0, "sTMP0", [128, 512], F32, attr="sTMP")

    def p0_sync(q):
        q = Chain(q, "hds")
        smalls = (
            (B.sONESR, dr["ones"]), (B.sONESC, dr["ones_col"]), (B.sIDF, dr["ident"]),
            (B.sC2F, dr["c2"]), (B.sBQKV, dr["bqkv2"]), (B.sBOUT, dr["bout2"]),
            (B.sSB1, dr["sb1_2"]), (B.sRB1, dr["rb1_2"]), (B.sSB22, dr["sb2_2"]),
            (B.sFB, dr["fbias"]), (B.sBQR, dr["bqvr"]),
        )
        for i, (out, src) in enumerate(smalls):
            ins = q.dma_start(out=out, in_=src)
            inc(ins, "hds", *(("csmall",) if i == len(smalls) - 1 else ()), by=16)
        wt(q, "dv", "modrow")
        ins = q.dma_start(out=dr["mod_in"], in_=B.sMROW)
        inc(ins, "hds", "modin", by=16)
        wtv(q, "cc1", 1)
        ins = q.dma_start(
            out=B.sMLD,
            in_=dr["mod_all"].rearrange("r (c p) -> (r c) p", p=128),
        )
        inc(ins, "hds", "modall", by=16)
        for kc in range(16):
            wt(q, "dv", ("kev", kc, 1))
            ins = q.dma_start(out=dr["k_in"][kc * 128:(kc + 1) * 128, :],
                              in_=B.sKST[:, (kc % 2) * T:(kc % 2) * T + T])
            inc(ins, "hds", ("kin", kc), by=16)
        wt(q, "ac", ("vevB", 47))
        for it in range(8):
            ins = q.dma_start(out=dr["v_in"][it * 128:(it + 1) * 128, :],
                              in_=B.sVAS[:, it * D:(it + 1) * D])
            inc(ins, "hds", ("vin", it), by=16)

    def p0_gpsimd(g):
        g = Chain(g, "ldg")
        ins = g.dma_start(
            out=B.sX.rearrange("p (c t) -> p c t", t=T),
            in_=dr["xT"].rearrange("(c p) t -> p c t", p=128),
        )
        inc(ins, "ldg", "xload", by=16)
        ins = g.dma_start(out=B.sONESCB, in_=dr["ones_col"])
        inc(ins, "ldg", "onescb", by=16)
        ins = g.dma_start(out=B.sIDB, in_=dr["ident"])
        inc(ins, "ldg", "idb", by=16)
        ins = g.dma_start(
            out=B.sCENTB.rearrange("p (c e) -> p c e", e=16),
            in_=dr["centT"].rearrange("(c p) e -> p c e", p=128),
        )
        inc(ins, "ldg", "cent", by=16)
        for kc in range(16):
            if kc >= 2:
                wt(g, "pe", ("mod", kc - 2))
            ins = g.dma_start(out=B.sWM[:, (kc % 2) * MODSL:(kc % 2 + 1) * MODSL],
                              in_=dr["wmod_sl"][kc * 128:(kc + 1) * 128, :])
            inc(ins, "ldg", ("wm", kc), by=16)
        wt(g, "hds", "modin")
        ins = g.collective_compute(
            "AllGather", ALU.bypass, replica_groups=[list(range(R))],
            ins=[dr["mod_in"].opt()], outs=[dr["mod_all"].opt()],
        )
        inc(ins, "cc1", "modag")
        for oc in range(OQ):
            if oc >= 2:
                wt(g, "pe", ("qkv", oc - 2))
            ins = g.dma_start(
                out=B.sWQ[:, (oc % 2) * DC * 128:(oc % 2 + 1) * DC * 128].rearrange(
                    "p (c f) -> p c f", f=128),
                in_=dr["wqkvT"][:, oc * 128:(oc + 1) * 128].rearrange(
                    "(c p) f -> p c f", p=128),
            )
            inc(ins, "ldg", ("wq", oc), by=16)
        wt(g, "hds", ("kin", 15))
        wt(g, "hds", ("vin", 7))
        ins = g.collective_compute(
            "AllGather", ALU.bypass, replica_groups=[list(range(R))],
            ins=[dr["k_in"].opt()], outs=[dr["k_all"].opt()],
        )
        inc(ins, "cc2", "kag")
        ins = g.collective_compute(
            "AllGather", ALU.bypass, replica_groups=[list(range(R))],
            ins=[dr["v_in"].opt()], outs=[dr["v_all"].opt()],
        )
        inc(ins, "cc3", "vag")

    def p0_tensor(t):
        wt(t, "ac", "silu")
        for kc in range(16):
            wt(t, "ldg", ("wm", kc))
            ins = None
            for os in range(3):
                ins = t.matmul(pp[os][0:1, :], B.sSCB[:, kc:kc + 1],
                               B.sWM[:, (kc % 2) * MODSL + os * 512:(kc % 2) * MODSL + (os + 1) * 512],
                               start=(kc == 0), stop=(kc == 15))
            inc(ins, "pe", ("mod", kc))
        wt(t, "hds", "modall")
        ins = t.transpose(pp[6][:, 0:96], B.sMLD, B.sIDF[0:96, 0:96])
        inc(ins, "pe", "modT")
        wt(t, "ldg", "onescb")
        wt(t, "dv", "modtev")
        for tch in range(2):
            for dc in range(DC):
                k = tch * DC + dc
                xs = B.sX[:, dc * T + tch * 512: dc * T + tch * 512 + 512]
                ins = t.matmul(pp[3 + tch][0:1, :], B.sONESCB, xs,
                               start=(dc == 0), stop=(dc == DC - 1))
                wt(t, "ac", ("xsq1", k))
                ins = t.matmul(pp[5 + tch][0:1, :], B.sONESC,
                               B.sXSQ[:, (k % 2) * 512:(k % 2) * 512 + 512],
                               start=(dc == 0), stop=(dc == DC - 1))
                inc(ins, "pe", ("sqmm1", k))
        inc(None, "pe", "stats1", by=0) if False else None
        wt(t, "dv", "ab1")
        wt(t, "dv", "modev")
        bank_ab1 = {(0, 0): 0, (0, 1): 1, (1, 0): 2, (1, 1): 5}
        ins = None
        for which in range(2):
            for tch in range(2):
                ins = t.matmul(pp[bank_ab1[(which, tch)]][:, :], B.sONESR[0:1, 0:128],
                               B.sSTAT[0:1, 3072 + which * 1024 + tch * 512: 3072 + which * 1024 + (tch + 1) * 512],
                               start=True, stop=True)
        inc(ins, "pe", "bc1")
        wt(t, "dv", "hdone1")
        for oc in range(OQ):
            wt(t, "ldg", ("wq", oc))
            wslab = B.sWQ[:, (oc % 2) * DC * 128:(oc % 2 + 1) * DC * 128]
            if oc < 32:
                for tch in range(2):
                    bank = (oc * 2 + tch) % 4
                    if oc >= 2:
                        if oc < 16:
                            wt(t, "dv", ("qev", oc - 2, tch))
                        elif oc == 16 or oc == 17:
                            wt(t, "dv", ("qev", oc - 2, tch))
                        else:
                            wt(t, "dv", ("kev", oc - 2 - 16, tch))
                    ins = None
                    for dc in range(DC):
                        ins = t.matmul(pp[bank][:, :], wslab[:, dc * 128:(dc + 1) * 128],
                                       B.sH[:, dc * T + tch * 512: dc * T + tch * 512 + 512],
                                       start=(dc == 0), stop=(dc == DC - 1))
                    inc(ins, "pe", ("qk", oc, tch), *((("qkv", oc),) if tch == 1 else ()))
            else:
                for it in range(8):
                    half = 4 if it < 4 else 5
                    qtr = it % 4
                    if it == 0:
                        wt(t, "ac", ("vevA", oc - 1)) if oc > 32 else wt(t, "dv", "hdone1")
                    if it == 4:
                        wt(t, "ac", ("vevB", oc - 1)) if oc > 32 else None
                        if oc == 32:
                            wt(t, "dv", "hdone1")
                    ins = None
                    for dc in range(DC):
                        ins = t.matmul(pp[half][:, qtr * 128:(qtr + 1) * 128],
                                       B.sH[:, dc * T + it * 128: dc * T + it * 128 + 128],
                                       wslab[:, dc * 128:(dc + 1) * 128],
                                       start=(dc == 0), stop=False)
                    ins = t.matmul(pp[half][:, qtr * 128:(qtr + 1) * 128],
                                   B.sONESR[0:1, 0:128],
                                   B.sBQR[0:1, oc * 128 - 4096:(oc + 1) * 128 - 4096],
                                   start=False, stop=True)
                    if it == 3:
                        inc(ins, "pe", ("qkA", oc))
                    elif it == 7:
                        inc(ins, "pe", ("qkB", oc), ("qkv", oc))
                    else:
                        inc(ins, "pe")
        inc(None, "pe", by=0) if False else None

    def p0_scalar(s):
        wt(s, "hds", "csmall")
        ins = s.activation(out=B.sSCB, in_=B.sC2F, func=ACTF.Silu)
        inc(ins, "ac", "silu")
        wt(s, "ldg", "xload")
        for tch in range(2):
            for dc in range(DC):
                k = tch * DC + dc
                if k >= 2:
                    wt(s, "pe", ("sqmm1", k - 2))
                ins = s.activation(out=B.sXSQ[:, (k % 2) * 512:(k % 2) * 512 + 512],
                                   in_=B.sX[:, dc * T + tch * 512: dc * T + tch * 512 + 512],
                                   func=ACTF.Square)
                inc(ins, "ac", ("xsq1", k))
        wt(s, "dv", "var1")
        ins = s.activation(out=B.sSTAT[0:1, 2048:3072], in_=B.sSTAT[0:1, 4096:5120],
                           func=ACTF.Sqrt)
        inc(ins, "ac", "sqrt1")
        for oc in range(32, OQ):
            wt(s, "pe", ("qkA", oc))
            if oc >= 34:
                wt(s, "hds", ("vin", 7)) if False else None
            ins = s.activation(
                out=B.sVAS.rearrange("p (a d) -> p a d", d=D)[:, 0:4, oc * 128 - 4096: oc * 128 - 4096 + 128],
                in_=pp[4].rearrange("p (a q) -> p a q", q=128),
                func=ACTF.Copy)
            inc(ins, "ac", ("vevA", oc))
            wt(s, "pe", ("qkB", oc))
            ins = s.activation(
                out=B.sVAS.rearrange("p (a d) -> p a d", d=D)[:, 4:8, oc * 128 - 4096: oc * 128 - 4096 + 128],
                in_=pp[5].rearrange("p (a q) -> p a q", q=128),
                func=ACTF.Copy)
            inc(ins, "ac", ("vevB", oc))

    def p0_vector(v):
        wt(v, "pe", ("mod", 15))
        ins = None
        for os in range(3):
            ins = v.tensor_copy(B.sMROW[0:1, os * 512:(os + 1) * 512], pp[os][0:1, :])
        inc(ins, "dv", "modrow", "modev")
        wt(v, "pe", "modT")
        ins = v.tensor_copy(B.sMODT, pp[6][:, 0:96])
        inc(ins, "dv", "modtev")
        wt(v, "hds", "csmall")
        v.tensor_scalar(B.sCA1, B.sMODT[:, 16:32], 1.0, None, ALU.add) if not dry else None
        v.tensor_scalar(B.sCB1, B.sMODT[:, 64:80], 1.0, None, ALU.add) if not dry else None
        for (src0, dstname) in ((32, "sGA"), (80, "sGB")):
            gsl = B.sMODT[:, src0:src0 + 16]
            dst = getattr(B, dstname) if not dry else None
            ins = v.tensor_single_scalar(B.sT16a, gsl, 0.0, ALU.is_ge) if not dry else None
            drain(v, ins)
            ins = v.tensor_scalar(B.sT16a, B.sT16a, 2.0, -1.0, ALU.mult, ALU.add) if not dry else None
            drain(v, ins)
            ins = v.tensor_scalar(B.sT16b, gsl, -1.0, None, ALU.mult) if not dry else None
            drain(v, ins)
            ins = v.tensor_max(B.sT16b, B.sT16b, gsl) if not dry else None
            drain(v, ins)
            ins = v.tensor_scalar_max(B.sT16b, B.sT16b, 0.25) if not dry else None
            drain(v, ins)
            ins = v.tensor_mul(dst, B.sT16a, B.sT16b) if not dry else None
            drain(v, ins)
        ins = v.tensor_mul(B.sGAB, B.sGA, B.sBOUT) if not dry else None
        drain(v, ins)
        ins = v.tensor_add(B.sSB2S, B.sSB22[:, 0:16], B.sSB22[:, 16:32]) if not dry else None
        drain(v, ins)
        ins = v.tensor_mul(B.sGBS2, B.sGB, B.sSB2S) if not dry else None
        inc(ins, "dv", "modderiv")
        wt(v, "pe", ("sqmm1", 31))
        if not dry:
            v.tensor_scalar(B.sSTAT[0:1, 0:512], pp[3][0:1, :], 1.0 / D, None, ALU.mult)
            v.tensor_scalar(B.sSTAT[0:1, 512:1024], pp[4][0:1, :], 1.0 / D, None, ALU.mult)
            v.tensor_scalar(B.sSTAT[0:1, 3072:3584], pp[5][0:1, :], 1.0 / D, None, ALU.mult)
            v.tensor_scalar(B.sSTAT[0:1, 3584:4096], pp[6][0:1, :], 1.0 / D, None, ALU.mult)
            v.tensor_mul(B.sSTAT[0:1, 4096:5120], B.sSTAT[0:1, 0:1024], B.sSTAT[0:1, 0:1024])
            v.tensor_sub(B.sSTAT[0:1, 1024:2048], B.sSTAT[0:1, 3072:4096], B.sSTAT[0:1, 4096:5120])
            v.tensor_scalar(B.sSTAT[0:1, 1024:2048], B.sSTAT[0:1, 1024:2048], EPS, None, ALU.add)
        ins = v.reciprocal(B.sSTAT[0:1, 4096:5120], B.sSTAT[0:1, 1024:2048]) if not dry else None
        inc(ins, "dv", "var1")
        wt(v, "ac", "sqrt1")
        if not dry:
            r0 = B.sSTAT[0:1, 2048:3072]
            v.tensor_mul(B.sSTAT[0:1, 5120:6144], r0, r0)
            v.tensor_mul(B.sSTAT[0:1, 5120:6144], B.sSTAT[0:1, 5120:6144], B.sSTAT[0:1, 1024:2048])
            v.tensor_scalar(B.sSTAT[0:1, 5120:6144], B.sSTAT[0:1, 5120:6144], -0.5, 1.5, ALU.mult, ALU.add)
            v.tensor_mul(B.sSTAT[0:1, 3072:4096], B.sSTAT[0:1, 5120:6144], r0)
            v.scalar_tensor_tensor(B.sSTAT[0:1, 4096:5120], B.sSTAT[0:1, 0:1024], -1.0,
                                   B.sSTAT[0:1, 3072:4096], ALU.mult, ALU.mult)
        ins = v.tensor_copy(B.sSTAT[0:1, 5120:5121], B.sSTAT[0:1, 3072:3073]) if not dry else None
        inc(ins, "dv", "ab1")
        wt(v, "pe", "bc1")
        bank_ab1 = {(0, 0): 0, (0, 1): 1, (1, 0): 2, (1, 1): 5}
        ins = None
        for dc in range(DC):
            for tch in range(2):
                xs = B.sX[:, dc * T + tch * 512: dc * T + tch * 512 + 512]
                if not dry:
                    v.tensor_mul(B.sTMP, xs, pp[bank_ab1[(0, tch)]][:, :])
                    v.tensor_add(B.sTMP, B.sTMP, pp[bank_ab1[(1, tch)]][:, :])
                ins = v.tensor_scalar(B.sH[:, dc * T + tch * 512: dc * T + tch * 512 + 512],
                                      B.sTMP, B.sCA1[:, dc:dc + 1],
                                      B.sMODT[:, dc:dc + 1], ALU.mult, ALU.add) if not dry else None
        inc(ins, "dv", "hdone1")
        for oc in range(32):
            for tch in range(2):
                wt(v, "pe", ("qk", oc, tch))
                bank = (oc * 2 + tch) % 4
                if oc < 16:
                    ins = v.tensor_scalar(B.sQT[:, oc * T + tch * 512: oc * T + tch * 512 + 512],
                                          pp[bank][:, :], B.sBQKV[:, oc:oc + 1], SCALE,
                                          ALU.add, ALU.mult) if not dry else None
                    inc(ins, "dv", ("qev", oc, tch))
                else:
                    kc = oc - 16
                    if kc >= 2:
                        wt(v, "hds", ("kin", kc - 2))
                    ins = v.tensor_scalar(B.sKST[:, (kc % 2) * T + tch * 512:(kc % 2) * T + tch * 512 + 512],
                                          pp[bank][:, :], B.sBQKV[:, oc:oc + 1], None,
                                          ALU.add) if not dry else None
                    inc(ins, "dv", ("kev", kc, tch))

    run_phase({"sync": p0_sync, "gpsimd": p0_gpsimd, "tensor": p0_tensor,
               "scalar": p0_scalar, "vector": p0_vector})
    if not dry:
        st0.close()

    st1 = ExitStack()
    if not dry:
        st1.__enter__()
    alloc(st1, "sCTX", [128, 8 * D], F32)
    alloc(st1, "sKB", [128, 2 * DC * 512], BF16)
    alloc(st1, "sVB", [128, 2 * 4 * D], BF16)
    alloc(st1, "sPRB", [128, 2 * 512], BF16)
    alloc(st1, "sPT", [128, 2 * 512], BF16)

    def p1_sync(q):
        q = Chain(q, "hds")
        wtv(q, "cc2", 1)
        wtv(q, "cc3", 1)
        wt(q, "pe", "p0pe")
        if "dbg_q" in dr:
            wt(q, "dv", "p0dv")
            ins = q.dma_start(out=dr["dbg_q"], in_=B.sQT)
            inc(ins, "hds", by=16)
        dbg_kb = "dbg_kb" in dr
        for jc in range(NJC):
            if jc >= 2:
                wt(q, "pe", ("fpv", (jc - 1) * 8 - 1))
            rr, jloc = jc // 2, (jc % 2) * 512
            ins = q.dma_start(
                out=B.sKB[:, (jc % 2) * DC * 512:(jc % 2 + 1) * DC * 512].rearrange(
                    "p (c t) -> p c t", t=512),
                in_=dr["k_all"][rr * D:(rr + 1) * D, jloc:jloc + 512].rearrange(
                    "(c p) t -> p c t", p=128),
            )
            inc(ins, "hds", ("kb", jc), by=16)
            ins = q.dma_start(
                out=B.sVB[:, (jc % 2) * 4 * D:(jc % 2 + 1) * 4 * D].rearrange(
                    "p (a d) -> p a d", d=D),
                in_=dr["v_all"][jc * 512:(jc + 1) * 512, :].rearrange(
                    "(a p) d -> p a d", p=128),
            )
            inc(ins, "hds", ("vb", jc), by=16)
            if dbg_kb and jc == 0:
                ins = q.dma_start(out=dr["dbg_kb"], in_=B.sKB[:, 0:DC * 512])
                inc(ins, "hds", by=16)
                ins = q.dma_start(out=dr["dbg_vb"], in_=B.sVB[:, 0:4 * D])
                inc(ins, "hds", by=16)
                wt(q, "dv", ("fstat", 0))
                ins = q.dma_start(out=dr["dbg_st0"], in_=B.sMN[:, 0:1])
                inc(ins, "hds", by=16)
                ins = q.dma_start(out=dr["dbg_st1"], in_=B.sMD[:, 0:1])
                inc(ins, "hds", by=16)
                wt(q, "ac", ("pexp", 0))
                ins = q.dma_start(out=dr["dbg_pr"], in_=B.sPRB[:, 0:512])
                inc(ins, "hds", by=16)
                ins = q.dma_start(out=dr["dbg_st2"], in_=B.sRS[:, 0:1])
                inc(ins, "hds", by=16)
                ins = q.dma_start(out=dr["dbg_st3"], in_=B.sAL[:, 0:1])
                inc(ins, "hds", by=16)
            if dbg_kb and jc == 1:
                wt(q, "ac", ("pexp", 8))
                ins = q.dma_start(out=dr["dbg_st4"], in_=B.sMN[:, 0:1])
                inc(ins, "hds", by=16)
                ins = q.dma_start(out=dr["dbg_st5"], in_=B.sAL[:, 0:1])
                inc(ins, "hds", by=16)
                ins = q.dma_start(out=dr["dbg_st6"], in_=B.sRS[:, 0:1])
                inc(ins, "hds", by=16)
                ins = q.dma_start(out=dr["dbg_st7"], in_=B.sPRB[:, 0:512])
                inc(ins, "hds", by=16)

    def p1_tensor(t):
        wt(t, "ac", "p0ac")
        wt(t, "dv", "p0dv")
        for m in range(NM + 1):
            jc, it = divmod(m, 8)
            if m < NM:
                if m < 8 or it == 0:
                    wt(t, "hds", ("kb", jc))
                if m >= 2:
                    wt(t, "ac", ("pexp", m - 2))
                ins = None
                for dc in range(DC):
                    ins = t.matmul(pp[m % 2][:, :],
                                   B.sQT[:, dc * T + it * 128: dc * T + it * 128 + 128],
                                   B.sKB[:, (jc % 2) * DC * 512 + dc * 512:(jc % 2) * DC * 512 + (dc + 1) * 512],
                                   start=(dc == 0), stop=(dc == DC - 1))
                inc(ins, "pe", ("fsc", m))
            if m >= 1:
                mp = m - 1
                jcp, itp = divmod(mp, 8)
                wt(t, "ac", ("pexp", mp))
                if mp >= 1:
                    wt(t, "dv", ("ptev", mp - 1))
                ins = None
                for qtr in range(4):
                    ins = t.transpose(pb[:, qtr * 128:(qtr + 1) * 128],
                                      B.sPRB[:, (mp % 2) * 512 + qtr * 128:(mp % 2) * 512 + (qtr + 1) * 128],
                                      B.sIDB)
                inc(ins, "pe", ("ftr", mp))
                wt(t, "dv", ("ptev", mp))
                if mp >= 1:
                    wt(t, "dv", ("fpvev", mp - 1))
                if mp < 8 or itp == 0:
                    wt(t, "hds", ("vb", jcp))
                for osl in range(4):
                    ins = None
                    for jsub in range(4):
                        ins = t.matmul(pp[2 + osl][:, :],
                                       B.sPT[:, (mp % 2) * 512 + jsub * 128:(mp % 2) * 512 + (jsub + 1) * 128],
                                       B.sVB[:, (jcp % 2) * 4 * D + jsub * D + osl * 512:(jcp % 2) * 4 * D + jsub * D + (osl + 1) * 512],
                                       start=(jsub == 0), stop=(jsub == 3))
                    if osl == 3:
                        inc(ins, "pe", ("fpv", mp))

    def p1_scalar(s):
        wt(s, "dv", "p0dv")
        for m in range(NM):
            jc, it = divmod(m, 8)
            cur = (B.sMN if jc % 2 == 0 else B.sMD)[:, it:it + 1]
            prv = (B.sMD if jc % 2 == 0 else B.sMN)[:, it:it + 1]
            wt(s, "dv", ("fstat", m))
            if m >= 2:
                wt(s, "pe", ("ftr", m - 2))
            ins = s.activation(out=B.sAL[:, it:it + 1], in_=prv, scale=-1.0,
                               bias=cur, func=ACTF.Exp)
            ins = s.activation(out=B.sPRB[:, (m % 2) * 512:(m % 2) * 512 + 512],
                               in_=pp[m % 2][:, :], func=ACTF.Exp,
                               bias=cur,
                               accum_out=B.sRS[:, it:it + 1])
            inc(ins, "ac", ("pexp", m))

    def p1_vector(v):
        wt(v, "pe", "p0pe")
        wt(v, "hds", ("vin", 7))
        if not dry:
            v.memset(B.sMN, 3.0e38)
            v.memset(B.sMD, 3.0e38)
            v.memset(B.sLL, 0.0)
            v.memset(B.sCTX, 0.0)
        ins = v.memset(B.sRS, 0.0) if not dry else None
        inc(ins, "dv", "p1init")
        for m in range(NM):
            jc, it = divmod(m, 8)
            cur = (B.sMN if jc % 2 == 0 else B.sMD)[:, it:it + 1]
            prv = (B.sMD if jc % 2 == 0 else B.sMN)[:, it:it + 1]
            wt(v, "pe", ("fsc", m))
            ins = v.reduce_max(B.sMJ[:, it:it + 1], pp[m % 2][:, :], axis=AXX, negate=True) if not dry else None
            drain(v, ins)
            ins = v.tensor_tensor(cur, B.sMJ[:, it:it + 1], prv, ALU.min) if not dry else None
            inc(ins, "dv", ("fstat", m))
            wt(v, "ac", ("pexp", m))
            ins = v.scalar_tensor_tensor(B.sLL[:, it:it + 1], B.sLL[:, it:it + 1],
                                         B.sAL[:, it:it + 1], B.sRS[:, it:it + 1],
                                         ALU.mult, ALU.add) if not dry else None
            inc(ins, "dv")
            wt(v, "pe", ("ftr", m))
            if m >= 2:
                wt(v, "pe", ("fpv", m - 1))
            ins = v.tensor_copy(B.sPT[:, (m % 2) * 512:(m % 2) * 512 + 512], pb[:, 0:512]) if not dry else None
            inc(ins, "dv", ("ptev", m))
            wt(v, "pe", ("fpv", m))
            ins = None
            for osl in range(4):
                ins = v.scalar_tensor_tensor(
                    B.sCTX[:, it * D + osl * 512: it * D + (osl + 1) * 512],
                    B.sCTX[:, it * D + osl * 512: it * D + (osl + 1) * 512],
                    B.sAL[:, it:it + 1], pp[2 + osl][:, :],
                    ALU.mult, ALU.add) if not dry else None
            inc(ins, "dv", ("fpvev", m))
        ins = v.reciprocal(B.sRL, B.sLL) if not dry else None
        drain(v, ins)
        ins = None
        for it in range(8):
            ins = v.tensor_scalar(B.sQT[:, it * D:(it + 1) * D],
                                  B.sCTX[:, it * D:(it + 1) * D],
                                  B.sRL[:, it:it + 1], None, ALU.mult) if not dry else None
        inc(ins, "dv", "ctxb")

    run_phase({"sync": p1_sync, "tensor": p1_tensor, "scalar": p1_scalar,
               "vector": p1_vector})
    if not dry:
        st1.close()

    st2a = ExitStack()
    if not dry:
        st2a.__enter__()
    alloc(st2a, "sX2", [128, DC * T], F32)
    alloc(st2a, "sW1B", [128, 2 * DC * 128], BF16)
    alloc(st2a, "sW2B", [128, 2 * FC * 128], BF16)
    if not dry:
        B.sWO = B.sW1B
    st2b = ExitStack()
    if not dry:
        st2b.__enter__()
    alloc(st2b, "sCTXT", [128, DC * T], BF16)
    alloc_pack(st2b, "sXT2", 1024, [("sXSQ2", 0, 1024), ("sTMP", 0, 512)], F32)
    alloc(st2b, "sSTAT2", [1, 6144], F32, attr="sSTAT")

    def p2_sync(q):
        q = Chain(q, "hds")
        wt(q, "dv", "ctxb")
        wt(q, "pe", ("fpv", NM - 1))
        ins = q.dma_start(
            out=B.sX2.rearrange("p (c t) -> p c t", t=T),
            in_=dr["xT"].rearrange("(c p) t -> p c t", p=128),
        )
        inc(ins, "hds", "xload2", by=16)
        if "dbg_x2" in dr:
            wt(q, "dv", ("opjev", 15, 1))
            ins = q.dma_start(out=dr["dbg_x2"], in_=B.sX2)
            inc(ins, "hds", "dbgx2", by=16)

    def p2_gpsimd(g):
        g = Chain(g, "ldg")
        wt(g, "dv", "ctxb")
        wt(g, "pe", ("fpv", NM - 1))
        for oc2 in range(16):
            if oc2 >= 2:
                wt(g, "pe", ("oproj", oc2 - 2))
            ins = g.dma_start(
                out=B.sWO[:, (oc2 % 2) * DC * 128:(oc2 % 2 + 1) * DC * 128].rearrange(
                    "p (c f) -> p c f", f=128),
                in_=dr["woutT"][:, oc2 * 128:(oc2 + 1) * 128].rearrange(
                    "(c p) f -> p c f", p=128),
            )
            inc(ins, "ldg", ("wo", oc2), by=16)
        for e in range(NE + 1):
            if e < NE:
                for fc in range(FC):
                    w = e * FC + fc
                    if w >= 2:
                        ep, fcp = divmod(w - 2, FC)
                        wt(g, "pe", ("hid", ep, fcp, 1))
                    else:
                        wt(g, "pe", ("oproj", 14 + w))
                    src = dr["sw1"][e] if e < 2 else dr["rw1"][e - 2]
                    lane = FFN_LANE[("w1", e, fc)]
                    ins = g.dma_lane(
                        lane,
                        out=B.sW1B[:, (w % 2) * DC * 128:(w % 2 + 1) * DC * 128].rearrange(
                            "p (c f) -> p c f", f=128),
                        in_=src[:, fc * 128:(fc + 1) * 128].rearrange("(c p) f -> p c f", p=128),
                    )
                    inc(ins, lane, ("w1", e, fc), by=16)
            if e >= 1:
                e2 = e - 1
                for dc in range(DC):
                    w = e2 * DC + dc
                    if w >= 2:
                        ep, dcp = divmod(w - 2, DC)
                        wt(g, "pe", ("w2", ep, dcp, 1))
                    src = dr["sw2"][e2] if e2 < 2 else dr["rw2"][e2 - 2]
                    lane = FFN_LANE[("w2l", e2, dc)]
                    ins = g.dma_lane(
                        lane,
                        out=B.sW2B[:, (w % 2) * FC * 128:(w % 2 + 1) * FC * 128].rearrange(
                            "p (c f) -> p c f", f=128),
                        in_=src[:, dc * 128:(dc + 1) * 128].rearrange("(c p) f -> p c f", p=128),
                    )
                    inc(ins, lane, ("w2l", e2, dc), by=16)

    def p2_tensor(t):
        wt(t, "dv", "ctxb")
        for it in range(8):
            for bb in range(4):
                bidx = it * 4 + bb
                if bidx >= 1:
                    wt(t, "dv", ("ctrv", bidx - 1))
                ins = None
                for k in range(4):
                    dc = bb * 4 + k
                    ins = t.transpose(pb[:, k * 128:(k + 1) * 128],
                                      B.sQT[:, it * D + dc * 128: it * D + (dc + 1) * 128],
                                      B.sIDB)
                inc(ins, "pe", ("ctr", bidx))
        wt(t, "dv", ("ctrv", 31))
        for oc2 in range(16):
            wt(t, "ldg", ("wo", oc2))
            wslab = B.sWO[:, (oc2 % 2) * DC * 128:(oc2 % 2 + 1) * DC * 128]
            for tch in range(2):
                bank = (oc2 * 2 + tch) % 4
                if oc2 >= 2:
                    wt(t, "dv", ("opjev", oc2 - 2, tch))
                ins = None
                for dc in range(DC):
                    ins = t.matmul(pp[bank][:, :], wslab[:, dc * 128:(dc + 1) * 128],
                                   B.sCTXT[:, dc * T + tch * 512: dc * T + tch * 512 + 512],
                                   start=(dc == 0), stop=(dc == DC - 1))
                inc(ins, "pe", ("opj", oc2, tch), *((("oproj", oc2),) if tch == 1 else ()))
        wt(t, "dv", ("opjev", 15, 1))
        for tch in range(2):
            for dc in range(DC):
                k = tch * DC + dc
                xs = B.sX2[:, dc * T + tch * 512: dc * T + tch * 512 + 512]
                ins = t.matmul(pp[4 + tch][0:1, :], B.sONESC, xs,
                               start=(dc == 0), stop=(dc == DC - 1))
                wt(t, "ac", ("xsq2", k))
                ins = t.matmul(pp[0 + tch][0:1, :], B.sONESC,
                               B.sXSQ2[:, (k % 2) * 512:(k % 2) * 512 + 512],
                               start=(dc == 0), stop=(dc == DC - 1))
                inc(ins, "pe", ("sqmm2", k))
        wt(t, "dv", "ab2")
        bank_ab2 = {(0, 0): 2, (0, 1): 3, (1, 0): 4, (1, 1): 5}
        ins = None
        for which in range(2):
            for tch in range(2):
                ins = t.matmul(pp[bank_ab2[(which, tch)]][:, :], B.sONESR[0:1, 0:128],
                               B.sSTAT[0:1, 3072 + which * 1024 + tch * 512: 3072 + which * 1024 + (tch + 1) * 512],
                               start=True, stop=True)
        inc(ins, "pe", "bc2", "p2pe")

    def p2_scalar(s):
        wt(s, "dv", "ctxb")
        wt(s, "hds", "xload2")
        for tch in range(2):
            for dc in range(DC):
                k = tch * DC + dc
                if k >= 2:
                    wt(s, "pe", ("sqmm2", k - 2))
                wt(s, "dv", ("opjev", dc, tch))
                ins = s.activation(out=B.sXSQ2[:, (k % 2) * 512:(k % 2) * 512 + 512],
                                   in_=B.sX2[:, dc * T + tch * 512: dc * T + tch * 512 + 512],
                                   func=ACTF.Square)
                inc(ins, "ac", ("xsq2", k))
        wt(s, "dv", "var2")
        ins = s.activation(out=B.sSTAT[0:1, 2048:3072], in_=B.sSTAT[0:1, 4096:5120],
                           func=ACTF.Sqrt)
        inc(ins, "ac", "sqrt2")

    def p2_vector(v):
        for bidx in range(32):
            it, bb = divmod(bidx, 4)
            wt(v, "pe", ("ctr", bidx))
            ins = v.tensor_copy(
                B.sCTXT.rearrange("p (c t) -> p c t", t=T)[:, bb * 4:(bb + 1) * 4, it * 128:(it + 1) * 128],
                pb[:, 0:512].rearrange("p (k q) -> p k q", q=128),
            ) if not dry else None
            inc(ins, "dv", ("ctrv", bidx))
        wt(v, "hds", "xload2")
        for dc in range(DC):
            ins = v.tensor_scalar(B.sX2[:, dc * T:(dc + 1) * T],
                                  B.sX2[:, dc * T:(dc + 1) * T],
                                  B.sGAB[:, dc:dc + 1], None, ALU.add) if not dry else None
            inc(ins, "dv", ("xbias", dc))
        for oc2 in range(16):
            for tch in range(2):
                wt(v, "pe", ("opj", oc2, tch))
                ins = v.scalar_tensor_tensor(
                    B.sX2[:, oc2 * T + tch * 512: oc2 * T + tch * 512 + 512],
                    pp[(oc2 * 2 + tch) % 4][:, :], B.sGA[:, oc2:oc2 + 1],
                    B.sX2[:, oc2 * T + tch * 512: oc2 * T + tch * 512 + 512],
                    ALU.mult, ALU.add) if not dry else None
                inc(ins, "dv", ("opjev", oc2, tch))
        wt(v, "pe", ("sqmm2", 31))
        if not dry:
            v.tensor_scalar(B.sSTAT[0:1, 0:512], pp[4][0:1, :], 1.0 / D, None, ALU.mult)
            v.tensor_scalar(B.sSTAT[0:1, 512:1024], pp[5][0:1, :], 1.0 / D, None, ALU.mult)
            v.tensor_scalar(B.sSTAT[0:1, 3072:3584], pp[0][0:1, :], 1.0 / D, None, ALU.mult)
            v.tensor_scalar(B.sSTAT[0:1, 3584:4096], pp[1][0:1, :], 1.0 / D, None, ALU.mult)
            v.tensor_mul(B.sSTAT[0:1, 4096:5120], B.sSTAT[0:1, 0:1024], B.sSTAT[0:1, 0:1024])
            v.tensor_sub(B.sSTAT[0:1, 1024:2048], B.sSTAT[0:1, 3072:4096], B.sSTAT[0:1, 4096:5120])
            v.tensor_scalar(B.sSTAT[0:1, 1024:2048], B.sSTAT[0:1, 1024:2048], EPS, None, ALU.add)
        ins = v.reciprocal(B.sSTAT[0:1, 4096:5120], B.sSTAT[0:1, 1024:2048]) if not dry else None
        inc(ins, "dv", "var2")
        wt(v, "ac", "sqrt2")
        if not dry:
            r0 = B.sSTAT[0:1, 2048:3072]
            v.tensor_mul(B.sSTAT[0:1, 5120:6144], r0, r0)
            v.tensor_mul(B.sSTAT[0:1, 5120:6144], B.sSTAT[0:1, 5120:6144], B.sSTAT[0:1, 1024:2048])
            v.tensor_scalar(B.sSTAT[0:1, 5120:6144], B.sSTAT[0:1, 5120:6144], -0.5, 1.5, ALU.mult, ALU.add)
            v.tensor_mul(B.sSTAT[0:1, 3072:4096], B.sSTAT[0:1, 5120:6144], r0)
            v.scalar_tensor_tensor(B.sSTAT[0:1, 4096:5120], B.sSTAT[0:1, 0:1024], -1.0,
                                   B.sSTAT[0:1, 3072:4096], ALU.mult, ALU.mult)
        ins = v.tensor_copy(B.sSTAT[0:1, 5120:5121], B.sSTAT[0:1, 3072:3073]) if not dry else None
        inc(ins, "dv", "ab2")
        wt(v, "pe", "bc2")
        if "dbg_x2" in dr:
            wt(v, "hds", "dbgx2")
        bank_ab2 = {(0, 0): 2, (0, 1): 3, (1, 0): 4, (1, 1): 5}
        ins = None
        for dc in range(DC):
            for tch in range(2):
                sl = slice(dc * T + tch * 512, dc * T + tch * 512 + 512)
                if not dry:
                    v.tensor_mul(B.sTMP, B.sX2[:, sl], pp[bank_ab2[(0, tch)]][:, :])
                    v.tensor_add(B.sTMP, B.sTMP, pp[bank_ab2[(1, tch)]][:, :])
                    v.tensor_scalar(B.sH[:, sl], B.sTMP, B.sCB1[:, dc:dc + 1],
                                    B.sMODT[:, 48 + dc:49 + dc], ALU.mult, ALU.add)
                    v.scalar_tensor_tensor(B.sX2[:, sl], B.sH[:, sl], B.sGB[:, dc:dc + 1],
                                           B.sX2[:, sl], ALU.mult, ALU.add)
                ins = v.tensor_scalar(B.sX2[:, sl], B.sX2[:, sl], B.sGBS2[:, dc:dc + 1],
                                      None, ALU.add) if not dry else None
        inc(ins, "dv", "hdone2", "p2dv")

    run_phase({"sync": p2_sync, "gpsimd": p2_gpsimd, "tensor": p2_tensor,
               "scalar": p2_scalar, "vector": p2_vector})
    if not dry:
        st2b.close()

    st3 = ExitStack()
    if not dry:
        st3.__enter__()
    alloc(st3, "sHID", [128, 2 * FC * T], BF16)
    alloc(st3, "sGBC", [128, 2 * T], BF16)
    alloc(st3, "sESEL", [16, 16 * 128], F32)
    alloc(st3, "sTKS", [128, 64], F32)
    alloc(st3, "sSCT", [128, 8 * 16], F32)
    alloc(st3, "sGT", [128, 8 * 16], F32)
    alloc(st3, "sSCO", [16, T], F32)
    if not dry:
        B.sGTT = B.sSCO

    def p3_sync(q):
        q = Chain(q, "hds")
        wt(q, "dv", "p2dv")
        if "dbg_h2" in dr:
            ins = q.dma_start(out=dr["dbg_fst"], in_=B.sMN)
            inc(ins, "hds", by=16)
            ins = q.dma_start(out=dr["dbg_fst2"], in_=B.sLL)
            inc(ins, "hds", by=16)
            ins = q.dma_start(out=dr["dbg_ctx"], in_=B.sQT)
            inc(ins, "hds", by=16)
            ins = q.dma_start(out=dr["dbg_mod"], in_=dr["mod_all"])
            inc(ins, "hds", by=16)
            ins = q.dma_start(out=dr["dbg_kall"], in_=dr["k_all"])
            inc(ins, "hds", by=16)
            ins = q.dma_start(out=dr["dbg_vall"], in_=dr["v_all"])
            inc(ins, "hds", by=16)
            ins = q.dma_start(out=dr["dbg_h2"], in_=B.sH)
            inc(ins, "hds", by=16)
            wt(q, "dv", ("gttv", 7))
            ins = q.dma_start(out=dr["dbg_gtt"], in_=B.sGTT)
            inc(ins, "hds", by=16)
        ins = q.dma_start(out=B.sESEL, in_=dr["esel"])
        inc(ins, "hds", "p3sm", by=16)
        wt(q, "pe", ("gbc", NE - 1))
        ins = q.dma_start(out=B.sESEL[0:16, 0:D], in_=dr["rb2"])
        inc(ins, "hds", "p3rb2", by=16)
        wt(q, "dv", ("b2ev", 15, 1))
        ins = q.dma_start(
            out=dr["y"].rearrange("(c p) t -> p c t", p=128),
            in_=B.sX2.rearrange("p (c t) -> p c t", t=T),
        )
        inc(ins, "hds", "ystore", by=16)
        wtv(q, "hds", S.n.get("hds", 0) if dry else S.at("hds", "ystore"))

    def p3_tensor(t):
        wt(t, "dv", "p2dv")
        for tch in range(2):
            ins = None
            for dc in range(DC):
                ins = t.matmul(pp[4 + tch][0:16, :], B.sCENTB[:, dc * 16:(dc + 1) * 16],
                               B.sH[:, dc * T + tch * 512: dc * T + tch * 512 + 512],
                               start=(dc == 0), stop=(dc == DC - 1))
            inc(ins, "pe", ("scot", tch))
        wt(t, "dv", "scob")
        for tt in range(8):
            if tt >= 1:
                wt(t, "dv", ("sctv", tt - 1))
            ins = t.transpose(pp[6][:, 0:16], B.sSCO[0:16, tt * 128:(tt + 1) * 128],
                              B.sIDF[0:16, 0:16])
            inc(ins, "pe", ("sct", tt))
        for tt in range(8):
            wt(t, "dv", ("top4", tt))
            if tt >= 1:
                wt(t, "dv", ("gttv", tt - 1))
            else:
                wt(t, "dv", ("sctv", 7))
            ins = t.transpose(pp[6][0:16, 0:128], B.sGT[:, tt * 16:(tt + 1) * 16],
                              B.sIDF)
            inc(ins, "pe", ("gtt", tt))
        for e in range(NE + 1):
            if e < NE:
                if e >= 2:
                    j = e - 2
                    wt(t, "dv", ("gttv", 7))
                    if e == 2:
                        wt(t, "ac", ("scos", 1))
                        wt(t, "hds", "p3sm")
                    else:
                        wt(t, "dv", ("gbcv", e - 1))
                    ins = None
                    for tch in range(2):
                        ins = t.matmul(pp[4 + tch][:, :], B.sESEL[0:16, j * 128:(j + 1) * 128],
                                       B.sGTT[0:16, tch * 512:(tch + 1) * 512],
                                       start=True, stop=True)
                    inc(ins, "pe", ("gbc", e))
                for fc in range(FC):
                    wt(t, FFN_LANE[("w1", e, fc)], ("w1", e, fc))
                    wslab = B.sW1B[:, ((e * FC + fc) % 2) * DC * 128:((e * FC + fc) % 2 + 1) * DC * 128]
                    for tch in range(2):
                        if e >= 1 or fc >= 1:
                            pfc = fc - 1 if fc >= 1 else FC - 1
                            pe_ = e if fc >= 1 else e - 1
                            wt(t, "ac", ("hrelu", pe_, pfc, tch))
                        ins = None
                        for dc in range(DC):
                            ins = t.matmul(pp[tch][:, :], wslab[:, dc * 128:(dc + 1) * 128],
                                           B.sH[:, dc * T + tch * 512: dc * T + tch * 512 + 512],
                                           start=(dc == 0), stop=(dc == DC - 1))
                        inc(ins, "pe", ("hid", e, fc, tch))
            if e >= 1:
                ep = e - 1
                if ep >= 2:
                    wt(t, "dv", ("hscale", ep, FC - 1))
                else:
                    wt(t, "ac", ("hrelu", ep, FC - 1, 1))
                for dc in range(DC):
                    wt(t, FFN_LANE[("w2l", ep, dc)], ("w2l", ep, dc))
                    wslab = B.sW2B[:, ((ep * DC + dc) % 2) * FC * 128:((ep * DC + dc) % 2 + 1) * FC * 128]
                    for tch in range(2):
                        if ep >= 1 or dc >= 1:
                            pdc = dc - 1 if dc >= 1 else DC - 1
                            pep = ep if dc >= 1 else ep - 1
                            wt(t, "dv", ("w2ev", pep, pdc, tch))
                        ins = None
                        for fc in range(FC):
                            ins = t.matmul(pp[2 + tch][:, :], wslab[:, fc * 128:(fc + 1) * 128],
                                           B.sHID[:, (ep % 2) * FC * T + fc * T + tch * 512:(ep % 2) * FC * T + fc * T + tch * 512 + 512],
                                           start=(fc == 0), stop=(fc == FC - 1))
                        inc(ins, "pe", ("w2", ep, dc, tch))
        wt(t, "dv", ("gbcv", NE - 1))
        wt(t, "hds", "p3rb2")
        for dc in range(DC):
            for tch in range(2):
                if dc >= 1:
                    wt(t, "dv", ("b2ev", dc - 1, tch))
                ins = t.matmul(pp[4 + tch][:, :], B.sESEL[0:16, dc * 128:(dc + 1) * 128],
                               B.sGTT[0:16, tch * 512:(tch + 1) * 512],
                               start=True, stop=True)
                inc(ins, "pe", ("b2", dc, tch))

    def p3_scalar(s):
        wt(s, "pe", "p2pe")
        wt(s, "dv", "p2dv")
        for tch in range(2):
            wt(s, "pe", ("scot", tch))
            ins = s.activation(out=B.sSCO[0:16, tch * 512:(tch + 1) * 512],
                               in_=pp[4 + tch][0:16, :], func=ACTF.Sigmoid)
            inc(ins, "ac", ("scos", tch))
        for e in range(NE):
            for fc in range(FC):
                for tch in range(2):
                    wt(s, "pe", ("hid", e, fc, tch))
                    if e >= 2 and fc == 0 and tch == 0:
                        wt(s, "pe", ("w2", e - 2, DC - 1, 1))
                    if e < 2:
                        bias = B.sSB1[:, e * FC + fc: e * FC + fc + 1]
                    else:
                        bias = B.sRB1[:, (e - 2) * FC + fc: (e - 2) * FC + fc + 1]
                    ins = s.activation(
                        out=B.sHID[:, (e % 2) * FC * T + fc * T + tch * 512:(e % 2) * FC * T + fc * T + tch * 512 + 512],
                        in_=pp[tch][:, :], func=ACTF.Relu, bias=bias)
                    inc(ins, "ac", ("hrelu", e, fc, tch))

    def p3_vector(v):
        wt(v, "pe", "p2pe")
        wt(v, "ac", ("scos", 1))
        ins = v.tensor_scalar(B.sSCO, B.sSCO, B.sFB, None, ALU.add) if not dry else None
        inc(ins, "dv", "scob")
        for tt in range(8):
            wt(v, "pe", ("sct", tt))
            ins = v.tensor_copy(B.sSCT[:, tt * 16:(tt + 1) * 16], pp[6][:, 0:16]) if not dry else None
            inc(ins, "dv", ("sctv", tt))
        for tt in range(8):
            ssl = B.sSCT[:, tt * 16:(tt + 1) * 16]
            gsl = B.sGT[:, tt * 16:(tt + 1) * 16]
            if not dry:
                w = B.sTKS[:, 0:16]
                mcol = B.sTKS[:, 16:17]
                ge = B.sTKS[:, 32:48]
            ins = v.tensor_copy(w, ssl) if not dry else None
            drain(v, ins)
            for rnd in range(3):
                ins = v.reduce_max(mcol, w, axis=AXX) if not dry else None
                drain(v, ins)
                ins = v.tensor_scalar(ge, w, mcol, None, ALU.is_ge) if not dry else None
                drain(v, ins)
                ins = v.scalar_tensor_tensor(w, ge, BIGNEG, w, ALU.mult, ALU.add) if not dry else None
                drain(v, ins)
            ins = v.reduce_max(mcol, w, axis=AXX) if not dry else None
            drain(v, ins)
            ins = v.tensor_scalar(ge, ssl, mcol, None, ALU.is_ge) if not dry else None
            drain(v, ins)
            ins = v.tensor_mul(gsl, ge, ssl) if not dry else None
            drain(v, ins)
            ins = v.reduce_sum(mcol, gsl, axis=AXX) if not dry else None
            drain(v, ins)
            ins = v.tensor_scalar_max(mcol, mcol, 1e-8) if not dry else None
            drain(v, ins)
            ins = v.reciprocal(B.sTKS[:, 17:18], mcol) if not dry else None
            drain(v, ins)
            ins = v.tensor_scalar(gsl, gsl, B.sTKS[:, 17:18], None, ALU.mult) if not dry else None
            inc(ins, "dv", ("top4", tt))
        for tt in range(8):
            wt(v, "pe", ("gtt", tt))
            ins = v.tensor_copy(B.sGTT[0:16, tt * 128:(tt + 1) * 128], pp[6][0:16, 0:128]) if not dry else None
            inc(ins, "dv", ("gttv", tt))
        for e in range(NE + 1):
            if e < NE and e >= 2:
                wt(v, "pe", ("gbc", e))
                ins = None
                for tch in range(2):
                    ins = v.tensor_copy(B.sGBC[:, (e % 2) * T + tch * 512:(e % 2) * T + tch * 512 + 512],
                                        pp[4 + tch][:, :]) if not dry else None
                inc(ins, "dv", ("gbcv", e))
                for fc in range(FC):
                    wt(v, "ac", ("hrelu", e, fc, 1))
                    hsl = B.sHID[:, (e % 2) * FC * T + fc * T:(e % 2) * FC * T + (fc + 1) * T]
                    ins = v.tensor_mul(hsl, hsl, B.sGBC[:, (e % 2) * T:(e % 2) * T + T]) if not dry else None
                    inc(ins, "dv", ("hscale", e, fc))
            elif e < NE:
                inc(None, "dv", ("gbcv", e), by=0)
            if e >= 1:
                ep = e - 1
                for dc in range(DC):
                    for tch in range(2):
                        wt(v, "pe", ("w2", ep, dc, tch))
                        sl = slice(dc * T + tch * 512, dc * T + tch * 512 + 512)
                        ins = v.scalar_tensor_tensor(
                            B.sX2[:, sl], pp[2 + tch][:, :], B.sGB[:, dc:dc + 1],
                            B.sX2[:, sl], ALU.mult, ALU.add) if not dry else None
                        inc(ins, "dv", ("w2ev", ep, dc, tch))
        for dc in range(DC):
            for tch in range(2):
                wt(v, "pe", ("b2", dc, tch))
                sl = slice(dc * T + tch * 512, dc * T + tch * 512 + 512)
                ins = v.scalar_tensor_tensor(
                    B.sX2[:, sl], pp[4 + tch][:, :], B.sGB[:, dc:dc + 1],
                    B.sX2[:, sl], ALU.mult, ALU.add) if not dry else None
                inc(ins, "dv", ("b2ev", dc, tch))

    run_phase({"sync": p3_sync, "tensor": p3_tensor, "scalar": p3_scalar,
               "vector": p3_vector})
    if not dry:
        st3.close()
        st2a.close()

    if not S.frozen:
        S.c[("pe", "p0pe")] = S.c[("pe", ("qkv", OQ - 1))]
        S.c[("dv", "p0dv")] = S.c[("dv", ("kev", 15, 1))]
        S.c[("ac", "p0ac")] = S.c[("ac", ("vevB", 47))]


def build_nc():
    nc = bass.Bass("TRN2", target_bir_lowering=False, debug=False,
                   detect_race_conditions=False)
    dr = {}

    def di(name, shape, dt=F32):
        dr[name] = nc.dram_tensor(name, shape, dt, kind="ExternalInput").ap()

    di("xT", [D, T])
    di("wqkvT", [D, 3 * D])
    di("woutT", [D, D])
    di("centT", [D, 16])
    di("c2", [128, 16])
    di("bqkv2", [128, OQ])
    di("bqvr", [1, D])
    di("bout2", [128, 16])
    di("sb1_2", [128, 22])
    di("rb1_2", [128, 176])
    di("sb2_2", [128, 32])
    di("rb2", [16, D])
    di("fbias", [16, 1])
    di("ones", [1, 128])
    di("ones_col", [128, 1])
    di("ident", [128, 128])
    di("esel", [16, 16 * 128])
    di("wmod_sl", [D, MODSL])
    di("sw1", [2, D, DFF])
    di("sw2", [2, DFF, D])
    di("rw1", [16, D, DFF])
    di("rw2", [16, DFF, D])
    dr["y"] = nc.dram_tensor("y", [D, T], F32, kind="ExternalOutput").ap()
    import os as _os
    if _os.environ.get("KDBG"):
        dr["dbg_mod"] = nc.dram_tensor("dbg_mod", [R, MODSL], F32, kind="ExternalOutput").ap()
        dr["dbg_kall"] = nc.dram_tensor("dbg_kall", [R * D, T], BF16, kind="ExternalOutput").ap()
        dr["dbg_vall"] = nc.dram_tensor("dbg_vall", [R * T, D], BF16, kind="ExternalOutput").ap()
        dr["dbg_h2"] = nc.dram_tensor("dbg_h2", [128, DC * T], BF16, kind="ExternalOutput").ap()
        dr["dbg_gtt"] = nc.dram_tensor("dbg_gtt", [16, T], F32, kind="ExternalOutput").ap()
        dr["dbg_x2"] = nc.dram_tensor("dbg_x2", [128, DC * T], F32, kind="ExternalOutput").ap()
        dr["dbg_q"] = nc.dram_tensor("dbg_q", [128, DC * T], BF16, kind="ExternalOutput").ap()
        dr["dbg_ctx"] = nc.dram_tensor("dbg_ctx", [128, DC * T], BF16, kind="ExternalOutput").ap()
        dr["dbg_kb"] = nc.dram_tensor("dbg_kb", [128, DC * 512], BF16, kind="ExternalOutput").ap()
        dr["dbg_vb"] = nc.dram_tensor("dbg_vb", [128, 4 * D], BF16, kind="ExternalOutput").ap()
        dr["dbg_sc"] = nc.dram_tensor("dbg_sc", [128, 512], F32, kind="ExternalOutput").ap()
        dr["dbg_pr"] = nc.dram_tensor("dbg_pr", [128, 512], BF16, kind="ExternalOutput").ap()
        for i in range(7):
            dr[f"dbg_st{i}"] = nc.dram_tensor(f"dbg_st{i}", [128, 1], F32, kind="ExternalOutput").ap()
        dr["dbg_st7"] = nc.dram_tensor("dbg_st7", [128, 512], BF16, kind="ExternalOutput").ap()
        dr["dbg_fst"] = nc.dram_tensor("dbg_fst", [128, 8], F32, kind="ExternalOutput").ap()
        dr["dbg_fst2"] = nc.dram_tensor("dbg_fst2", [128, 8], F32, kind="ExternalOutput").ap()
    dr["mod_in"] = nc.dram_tensor("mod_in", [1, MODSL], F32, kind="Internal").ap()
    dr["mod_all"] = nc.dram_tensor("mod_all", [R, MODSL], F32, kind="Internal").ap()
    dr["k_in"] = nc.dram_tensor("k_in", [D, T], BF16, kind="Internal").ap()
    dr["k_all"] = nc.dram_tensor("k_all", [R * D, T], BF16, kind="Internal").ap()
    dr["v_in"] = nc.dram_tensor("v_in", [T, D], BF16, kind="Internal").ap()
    dr["v_all"] = nc.dram_tensor("v_all", [R * T, D], BF16, kind="Internal").ap()

    with ExitStack() as ctx:
        sems = {n: ctx.enter_context(nc.semaphore(n))
                for n in ("pe", "dv", "ac", "hds", "ldg", "lgb", "cc1", "cc2", "cc3")}
        S = Sched()
        emit(nc, ctx, dr, sems, S, dry=True)
        S.frozen = True
        with nc.Block() as blk:
            emit(nc, ctx, dr, sems, S, dry=False, blk=blk)
    return nc


def kernel(L0_attn_hidden_states, L0_attn_w_qkv, L0_attn_b_qkv, L0_attn_w_out,
           L0_attn_b_out, L0_ffn_shared_w1, L0_ffn_shared_b1, L0_ffn_shared_w2,
           L0_ffn_shared_b2, L0_ffn_routed_w1, L0_ffn_routed_b1, L0_ffn_routed_w2,
           L0_ffn_routed_b2, L0_ffn_centroids, L0_ffn_bias, L0_c, L0_wmod):
    f = np.float32
    x = np.asarray(L0_attn_hidden_states, f)
    common = {
        "wqkvT": np.ascontiguousarray(np.asarray(L0_attn_w_qkv, f).T),
        "woutT": np.ascontiguousarray(np.asarray(L0_attn_w_out, f).T),
        "centT": np.ascontiguousarray(np.asarray(L0_ffn_centroids, f).T),
        "c2": np.ascontiguousarray(np.asarray(L0_c, f).reshape(16, 128).T),
        "bqkv2": np.ascontiguousarray(np.asarray(L0_attn_b_qkv, f).reshape(OQ, 128).T),
        "bqvr": np.ascontiguousarray(np.asarray(L0_attn_b_qkv, f)[4096:6144].reshape(1, D)),
        "bout2": np.ascontiguousarray(np.asarray(L0_attn_b_out, f).reshape(16, 128).T),
        "sb1_2": np.ascontiguousarray(
            np.asarray(L0_ffn_shared_b1, f).reshape(2, FC, 128).transpose(2, 0, 1).reshape(128, 22)),
        "rb1_2": np.ascontiguousarray(
            np.asarray(L0_ffn_routed_b1, f).reshape(16, FC, 128).transpose(2, 0, 1).reshape(128, 176)),
        "sb2_2": np.ascontiguousarray(
            np.asarray(L0_ffn_shared_b2, f).reshape(2, 16, 128).transpose(2, 0, 1).reshape(128, 32)),
        "rb2": np.ascontiguousarray(np.asarray(L0_ffn_routed_b2, f)),
        "fbias": np.ascontiguousarray(np.asarray(L0_ffn_bias, f).reshape(16, 1)),
        "ones": np.ones((1, 128), f),
        "ones_col": np.ones((128, 1), f),
        "ident": np.eye(128, dtype=f),
        "esel": np.ascontiguousarray(
            np.eye(16, dtype=f)[:, :, None].repeat(128, axis=2).reshape(16, 16 * 128)),
        "sw1": np.ascontiguousarray(np.asarray(L0_ffn_shared_w1, f)),
        "sw2": np.ascontiguousarray(np.asarray(L0_ffn_shared_w2, f)),
        "rw1": np.ascontiguousarray(np.asarray(L0_ffn_routed_w1, f)),
        "rw2": np.ascontiguousarray(np.asarray(L0_ffn_routed_w2, f)),
    }
    wmod = np.asarray(L0_wmod, f)
    in_maps = []
    for r in range(R):
        m = dict(common)
        m["xT"] = np.ascontiguousarray(x[0, r * T:(r + 1) * T, :].T)
        m["wmod_sl"] = np.ascontiguousarray(wmod[:, r * MODSL:(r + 1) * MODSL])
        in_maps.append(m)

    nc = build_nc()
    res = run_bass_kernel_spmd(nc, in_maps, core_ids=list(range(R)))
    out = np.concatenate([np.asarray(res.results[r]["y"]).T for r in range(R)], axis=0)
    return np.ascontiguousarray(out.reshape(1, SG, D), f)
